# version 24
# baseline (speedup 1.0000x reference)
import numpy as np
import ml_dtypes
import concourse.bass as bass
import concourse.mybir as mybir
from concourse.bass_utils import run_bass_kernel_spmd
from contextlib import ExitStack

F32 = mybir.dt.float32
BF16 = mybir.dt.bfloat16
AF = mybir.ActivationFunctionType
ALU = mybir.AluOpType
AX = mybir.AxisListType
BF = ml_dtypes.bfloat16

D = 1024
NC = 8
SLICE = 2048
WARM = 256
TOK = SLICE + WARM
NT = TOK // 128
NWARM = WARM // 128
NMAIN = SLICE // 128
NCH = D // 128
CAP = 2048.0
EPS = 1e-5
E = 8
BLKS = [(0, 512), (512, 512), (1024, 512), (1536, 512), (2048, 256)]
YW = 3


def _incby(sem):
    return 16 if sem[:2] in ("LD", "WL", "XF", "ST", "XP") else 1


class Sched:

    def __init__(self):
        self.ops = []
        self.cnt = {}
        self.wait_claims = {}

    def val(self, sem):
        return self.cnt.get(sem, 0)

    def add(self, eng, fn, waits=(), inc=None):
        new_cnt = None
        if inc is not None:
            new_cnt = self.cnt.get(inc, 0) + _incby(inc)
        rep = (inc, new_cnt) if inc is not None else None
        fixed = []
        for base, v in waits:
            if v <= 0:
                continue
            guard = 0
            while not (_incby(base) == 16 or base == "CC"):
                claim = self.wait_claims.get((base, v))
                if claim is None:
                    self.wait_claims[(base, v)] = (eng, rep)
                    break
                if claim[0] == eng:
                    break
                if claim[1] is None:
                    break
                base, v = claim[1]
                guard += 1
                assert guard < 64, "wait chaining loop"
            fixed.append((base, v))
        if inc is not None:
            self.cnt[inc] = new_cnt
        self.ops.append((eng, fn, tuple(fixed), inc))

    def variant_plan(self):
        plan = {}
        for eng, _, waits, _ in self.ops:
            for sem_name, v in waits:
                if _incby(sem_name) == 16 or sem_name == "CC":
                    key = (sem_name, None)
                else:
                    key = (sem_name, eng)
                plan.setdefault(key, set()).add(v)
        return plan

    def sem_names(self):
        names = {"V.self", "S.self"}
        for (base, eng), _ in self.variant_plan().items():
            names.add(base if eng is None else f"{base}.{eng}")
        return sorted(names)

    def emit_body(self, nc, sems, eng_name):
        eng = {"sync": nc.sync, "gpsimd": nc.gpsimd, "scalar": nc.scalar,
               "vector": nc.vector, "tensor": nc.tensor}[eng_name]
        plan = self.variant_plan()
        variants = {}
        for (base, ceng), vals in plan.items():
            if ceng is not None:
                variants.setdefault(base, []).append((f"{base}.{ceng}", vals))
        self_chain = eng_name in ("vector", "scalar")
        selfsem = {"vector": "V.self", "scalar": "S.self"}.get(eng_name)
        selfval = 0
        cnt = 0
        due = []
        pend = {}
        prev_attach = None
        seen = {}
        for name, fn, waits, inc in self.ops:
            if name != eng_name:
                continue
            wlist = []
            for sem_name, v in waits:
                if not (_incby(sem_name) == 16 or sem_name == "CC"):
                    sem_name = f"{sem_name}.{eng_name}"
                wlist.append((sem_name, v))
            if self_chain and prev_attach is not None:
                wlist.append(prev_attach)
            for sem_name, v in wlist:
                if seen.get(sem_name, 0) < v:
                    eng.wait_ge(sems[sem_name], v)
                    seen[sem_name] = v
            ins = fn(eng)
            if inc is None:
                continue
            step = _incby(inc)
            if step == 16 or inc == "CC":
                ins.then_inc(sems[inc], step)
                continue
            cnt += step
            for vs_name, vals in variants.get(inc, ()):
                pend[vs_name] = pend.get(vs_name, 0) + step
                if cnt in vals and vs_name not in due:
                    due.append(vs_name)
            if due:
                vs = due.pop(0)
                ins.then_inc(sems[vs], pend[vs])
                prev_attach = (vs, cnt)
                pend[vs] = 0
            elif self_chain:
                selfval += 1
                ins.then_inc(sems[selfsem], 1)
                prev_attach = (selfsem, selfval)
        assert not due, f"unflushed sem variants on {eng_name}: {due}"


def build_nc(upto=99):
    nc = bass.Bass("TRN2", target_bir_lowering=False, detect_race_conditions=False)

    xs = nc.dram_tensor("xs", [TOK, D], F32, kind="ExternalInput").ap()
    w_in = nc.dram_tensor("w_in", [D, D], F32, kind="ExternalInput").ap()
    w_out = nc.dram_tensor("w_out", [D, D], F32, kind="ExternalInput").ap()
    w_e = nc.dram_tensor("w_e", [D, D], F32, kind="ExternalInput").ap()
    w_g = nc.dram_tensor("w_g", [D, E], F32, kind="ExternalInput").ap()
    b_in = nc.dram_tensor("b_in", [1, D], F32, kind="ExternalInput").ap()
    b_in_c = nc.dram_tensor("b_in_c", [128, E], F32, kind="ExternalInput").ap()
    b_out = nc.dram_tensor("b_out", [1, D], F32, kind="ExternalInput").ap()
    b_e = nc.dram_tensor("b_e", [1, D], F32, kind="ExternalInput").ap()
    log_a = nc.dram_tensor("log_a", [128, E], F32, kind="ExternalInput").ap()
    b_p = nc.dram_tensor("b_p", [128, E], F32, kind="ExternalInput").ap()
    c_p = nc.dram_tensor("c_p", [128, E], F32, kind="ExternalInput").ap()
    gam_bc = nc.dram_tensor("gam_bc", [128, D], F32, kind="ExternalInput").ap()
    bet_bc = nc.dram_tensor("bet_bc", [128, D], F32, kind="ExternalInput").ap()
    identb = nc.dram_tensor("identb", [128, 128], BF16, kind="ExternalInput").ap()
    slob = nc.dram_tensor("slob", [128, 128], BF16, kind="ExternalInput").ap()
    onescol_b = nc.dram_tensor("onescol_b", [128, 1], BF16, kind="ExternalInput").ap()
    onesrow_b = nc.dram_tensor("onesrow_b", [1, 128], BF16, kind="ExternalInput").ap()
    onesrow32 = nc.dram_tensor("onesrow32", [1, 128], F32, kind="ExternalInput").ap()
    browb = nc.dram_tensor("browb", [1, TOK], BF16, kind="ExternalInput").ap()
    sel8 = nc.dram_tensor("sel8", [8, 1], F32, kind="ExternalInput").ap()
    y_out = nc.dram_tensor("y_out", [SLICE, D], F32, kind="ExternalOutput").ap()
    cc_in = nc.dram_tensor("cc_in", [1, E], F32)
    cc_out = nc.dram_tensor("cc_out", [NC, E], F32)

    es = ExitStack()
    with es:
        def sb(name, shape, dt):
            return nc.alloc_sbuf_tensor(name, shape, dt)

        xnfm = sb("xnfm", [128, NCH * TOK], BF16)
        winT = sb("winT", [128, NCH * D], BF16)
        woutT = sb("woutT", [128, NCH * D], BF16)
        yw = sb("yw", [128, YW * NCH * 512], BF16)
        gbc = sb("gbc", [128, D], F32)
        bbc = sb("bbc", [128, D], F32)
        wg32 = sb("wg32", [128, NCH * E], F32)
        wghl = sb("wghl", [128, NCH * 2 * E], BF16)
        wghf = sb("wghf", [128, NCH * E], F32)
        wgl32 = sb("wgl32", [128, NCH * E], F32)
        qtmp = sb("qtmp", [128, 3 * E], F32)
        qall = sb("qall", [128, 3 * E * NMAIN], F32)
        mball = sb("mball", [128, E * NMAIN], BF16)
        idt = sb("idt", [128, 128], BF16)
        slo = sb("slo", [128, 128], BF16)
        onc = sb("onc", [128, 1], BF16)
        onrb = sb("onrb", [1, 128], BF16)
        onr32 = sb("onr32", [1, 128], F32)
        brw = sb("brw", [1, TOK], BF16)
        sel = sb("sel", [8, 1], F32)
        binb = sb("binb", [1, D], BF16)
        boutb = sb("boutb", [1, D], BF16)
        beb = sb("beb", [1, D], BF16)
        la = sb("la", [128, E], F32)
        bpc = sb("bpc", [128, E], F32)
        cpc = sb("cpc", [128, E], F32)
        spc = sb("spc", [128, E], F32)
        dcol = sb("dcol", [128, E], F32)
        stcol = sb("stcol", [128, E], F32)
        binc = sb("binc", [128, E], F32)
        binB = sb("binB", [128, E], F32)
        m16 = sb("m16", [128, E * NMAIN], F32)
        q16 = sb("q16", [128, E * NMAIN], F32)
        ex16 = sb("ex16", [128, E * NMAIN], F32)
        nmx16 = sb("nmx16", [128, NMAIN], F32)
        oc16 = sb("oc16", [128, NMAIN], F32)
        op16 = sb("op16", [128, NMAIN], F32)
        gate16 = sb("gate16", [128, NMAIN], F32)
        es16 = sb("es16", [128, NMAIN], F32)
        kg16 = sb("kg16", [128, NMAIN], F32)
        sum16 = sb("sum16", [128, NMAIN], F32)
        keep16 = sb("keep16", [128, NMAIN], F32)
        mu16 = sb("mu16", [128, NMAIN], F32)
        var16 = sb("var16", [128, NMAIN], F32)
        sd16 = sb("sd16", [128, NMAIN], F32)
        inv16 = sb("inv16", [128, NMAIN], F32)
        nmi16 = sb("nmi16", [128, NMAIN], F32)
        carry = sb("carry", [1, E], F32)
        snapc = sb("snapc", [1, E * NMAIN], F32)
        prow = sb("prow", [1, E], F32)
        cprow = sb("cprow", [1, E * NMAIN], F32)
        gth = sb("gth", [8, E], F32)
        mbuf = sb("mbuf", [128, E], BF16)
        cnts = sb("cnts", [128, E], F32)
        sq = sb("sq", [128, D], F32)
        stat = sb("stat", [128, 24], F32)

        AREN = 83968
        base = nc.bump_sbuf(AREN)
        assert base is not None, "SBUF arena allocation failed"
        off0 = base[0]

        def at(name, off, shape, dt):
            return nc.alloc_sbuf_tensor_at(name, shape, dt, offset=off0 + off)

        xb = [at("xb0", 0, [128, D], F32), at("xb1", 4096, [128, D], F32)]
        xhi = [at("xhi0", 8192, [128, D], BF16), at("xhi1", 10240, [128, D], BF16)]
        xlo = [at("xlo0", 12288, [128, D], BF16), at("xlo1", 14336, [128, D], BF16)]
        xhi32 = at("xh32", 16384, [128, D], F32)
        xlo32 = at("xl32", 20480, [128, D], F32)
        wesb = at("wesb", 8192, [128, NCH * D], BF16)
        xfl = [at("xfl0", 24576, [128, NCH * 128], BF16)]
        d1 = [at("d1a", 26624, [128, 512], F32), at("d1b", 28672, [128, 512], F32)]
        ssb = [at("ssb0", 30720, [128, 512], F32), at("ssb1", 81920, [128, 512], F32)]
        NOUT = 16
        outb = at("outb", 32768, [128, NOUT * D], BF16)
        sbs = [at("sbs0", 65536, [128, D], F32), at("sbs1", 69632, [128, D], F32)]
        gbuf = [at("gb0", 73728, [128, D], F32), at("gb1", 77824, [128, D], F32)]
        xre = xb
        wst = at("wst", 32768, [128, NCH * D], BF16)

        P01 = nc.alloc_psum_tensor("P01", [128, 1024], F32)
        P2 = nc.alloc_psum_tensor("P2", [128, 512], F32)
        P3 = nc.alloc_psum_tensor("P3", [128, 512], F32)
        P4 = nc.alloc_psum_tensor("P4", [128, 512], F32)
        P5 = nc.alloc_psum_tensor("P5", [128, 512], F32)
        P67 = nc.alloc_psum_tensor("P67", [128, 1024], F32)

        sch = Sched()

        def dma(eng_name, dst, src, sem, waits=()):
            sch.add(eng_name, lambda e, d=dst, s=src: e.dma_start(out=d, in_=s),
                    waits=waits, inc=sem)

        dma("sync", gbc[:], gam_bc, "LDC")
        dma("sync", bbc[:], bet_bc, "LDC")
        dma("sync", idt[:], identb, "LDC")
        dma("sync", slo[:], slob, "LDC")
        dma("sync", onc[:], onescol_b, "LDC")
        dma("sync", onrb[:], onesrow_b, "LDC")
        dma("sync", onr32[:], onesrow32, "LDC")
        dma("sync", brw[:], browb, "LDC")
        dma("sync", sel[:], sel8, "LDC")
        dma("sync", la[:], log_a, "LDC")
        dma("sync", bpc[:], b_p, "LDC")
        dma("sync", cpc[:], c_p, "LDC")
        dma("sync", binc[:], b_in_c, "LDC")
        dma("sync", wg32[:].rearrange("p (c j) -> p c j", j=E),
            w_g.rearrange("(c p) j -> p c j", p=128), "LDC")
        LD_CONST = sch.val("LDC")
        dma("gpsimd", binb[:], b_in, "WL")
        dma("gpsimd", boutb[:], b_out, "WL")
        dma("gpsimd", beb[:], b_e, "WL")
        dma("gpsimd", wst[:].rearrange("p (c d) -> p c d", d=D),
            w_in.rearrange("(c p) d -> p c d", p=128), "WL")
        WL_WIN = sch.val("WL")

        wconst = [("LDC", LD_CONST)]
        sch.add("scalar", lambda e: e.activation(spc[:], la[:], AF.Exp), waits=wconst, inc="S")
        s_e1 = sch.val("S")
        sch.add("vector", lambda e: e.tensor_scalar(spc[:], spc[:], 1.0, None, ALU.add),
                waits=[("S", s_e1)], inc="V")
        v_e1 = sch.val("V")
        sch.add("scalar", lambda e: e.activation(spc[:], spc[:], AF.Ln),
                waits=[("V", v_e1)], inc="S")
        s_ln = sch.val("S")
        sch.add("scalar", lambda e: e.activation(dcol[:], spc[:], AF.Exp, scale=-1.0),
                waits=[("S", s_ln)], inc="S")
        S_DECAY = sch.val("S")
        wghl3 = wghl[:].rearrange("p (c j) -> p c j", j=2 * E)
        wg32_3 = wg32[:].rearrange("p (c j) -> p c j", j=E)
        wghf3 = wghf[:].rearrange("p (c j) -> p c j", j=E)
        wgl32_3 = wgl32[:].rearrange("p (c j) -> p c j", j=E)
        sch.add("vector", lambda e: e.tensor_copy(wghl3[:, :, 0:E], wg32_3), waits=wconst, inc="V")
        v1 = sch.val("V")
        sch.add("vector", lambda e: e.tensor_copy(wghf3, wghl3[:, :, 0:E]),
                waits=[("V", v1)], inc="V")
        v1b = sch.val("V")
        sch.add("vector", lambda e: e.tensor_tensor(wgl32[:], wg32[:], wghf[:], ALU.subtract),
                waits=[("V", v1b)], inc="V")
        sch.add("vector", lambda e: e.tensor_copy(wghl3[:, :, E:2 * E], wgl32_3), inc="V")
        V_WG = sch.val("V")
        sch.add("vector", lambda e: e.memset(stcol[:], 0.0), inc="V")
        sch.add("vector", lambda e: e.memset(carry[:], 0.0), inc="V")
        sch.add("vector", lambda e: e.tensor_mul(binB[:], binc[:], bpc[:]), waits=wconst, inc="V")
        V_ST0 = sch.val("V")

        def w_transpose(dst_sb, wait_first, prev_ev=None):
            ev = {}
            for dc in range(NCH):
                slot = dc % 2
                half_ev = 0
                for er in range(NCH):
                    w = []
                    if er == 0:
                        w = list(wait_first)
                        if dc >= 2:
                            w.append(("S", ev[dc - 2]))
                        elif prev_ev is not None:
                            w.append(("S", prev_ev[6 + slot]))
                    elif er == 4:
                        w = [("S", half_ev)]
                    def mm(e, er=er, dc=dc, slot=slot):
                        return e.matmul(P67[:, slot * 512 + (er % 4) * 128:
                                            slot * 512 + ((er % 4) + 1) * 128],
                                        wst[:, er * D + dc * 128: er * D + (dc + 1) * 128],
                                        idt[:], start=True, stop=True)
                    sch.add("tensor", mm, waits=w, inc="T")
                    if er in (3, 7):
                        t_done = sch.val("T")
                        half = 0 if er == 3 else 1
                        def evc(e, dc=dc, slot=slot, dst_sb=dst_sb, half=half):
                            return e.activation(
                                dst_sb[:, dc * D + half * 512: dc * D + half * 512 + 512],
                                P67[:, slot * 512:(slot + 1) * 512], AF.Copy)
                        sch.add("scalar", evc, waits=[("T", t_done)], inc="S")
                        half_ev = sch.val("S")
                ev[dc] = sch.val("S")
            return ev

        ev_win = w_transpose(winT, [("WL", WL_WIN), ("LDC", LD_CONST)])
        dma("gpsimd", wst[:].rearrange("p (c d) -> p c d", d=D),
            w_out.rearrange("(c p) d -> p c d", p=128), "WL",
            waits=[("S", ev_win[7]), ("WL", WL_WIN)])
        WL_WOUT = sch.val("WL")
        ev_wout = w_transpose(woutT, [("WL", WL_WOUT)], prev_ev=ev_win)
        S_WIN = max(ev_win[6], ev_win[7])
        S_WPREP = max(ev_wout[6], ev_wout[7])

        COL = {k: i for i, k in enumerate(
            ["s1", "mu", "mu2", "var", "sd", "inv", "nmui", "s2"])}

        def st(col, sl):
            return stat[:, sl * 8 + COL[col]:sl * 8 + COL[col] + 1]

        x_free = [("V", 0)] * 2
        xhi_free = [("T", 0)] * 2
        xlo_free = [("T", 0)] * 2
        xfl_free = [("T", 0)]
        evA_hi = {}
        q_free = [("V", 0)]
        ps_hi_free = [("S", 0)]
        ps_lo_free = [("S", 0)]
        t_tot_prev = [0]

        stat_wfree = [("S", 0), ("S", 0)]
        front_nmui = {}

        def a_front(t):
            sl = t % 2
            ldx = f"LDX{sl}"
            sch.add("sync", lambda e, t=t, sl=sl: e.dma_start(
                out=xb[sl][:], in_=xs[t * 128:(t + 1) * 128, :]),
                waits=[x_free[sl]], inc=ldx)
            wx = [(ldx, sch.val(ldx))]
            sch.add("vector", lambda e, sl=sl: e.reduce_sum(st("s1", sl), xb[sl][:], axis=AX.X),
                    waits=wx, inc="V")
            v_s1 = sch.val("V")
            sch.add("scalar", lambda e, sl=sl: e.activation(sq[:], xb[sl][:], AF.Square,
                                                            accum_out=st("s2", sl)),
                    waits=wx + [stat_wfree[sl]], inc="S")
            s_s2 = sch.val("S")
            sch.add("vector", lambda e, sl=sl: e.tensor_scalar(st("mu", sl), st("s1", sl),
                                                               1.0 / D, None, ALU.mult),
                    waits=[("V", v_s1)], inc="V")
            sch.add("vector", lambda e, sl=sl: e.tensor_mul(st("mu2", sl), st("mu", sl),
                                                            st("mu", sl)), inc="V")
            sch.add("vector", lambda e, sl=sl: e.tensor_scalar(st("var", sl), st("s2", sl),
                                                               1.0 / D, EPS, ALU.mult, ALU.add),
                    waits=[("S", s_s2)], inc="V")
            sch.add("vector", lambda e, sl=sl: e.tensor_sub(st("var", sl), st("var", sl),
                                                            st("mu2", sl)), inc="V")
            v_var = sch.val("V")
            sch.add("scalar", lambda e, sl=sl: e.activation(st("sd", sl), st("var", sl), AF.Ln),
                    waits=[("V", v_var)], inc="S")
            sch.add("scalar", lambda e, sl=sl: e.activation(st("inv", sl), st("sd", sl), AF.Exp,
                                                            scale=-0.5), inc="S")
            s_inv = sch.val("S")
            sch.add("vector", lambda e, sl=sl: e.tensor_scalar(st("nmui", sl), st("mu", sl),
                                                               -1.0, st("inv", sl),
                                                               ALU.mult, ALU.mult),
                    waits=[("S", s_inv)], inc="V")
            front_nmui[t] = sch.val("V")

        lo_state = {}

        def a_mid(t):
            sl = t % 2
            main = t >= NWARM
            tm = t - NWARM
            v_nmui = front_nmui[t]
            sch.add("scalar", lambda e, sl=sl: e.activation(xhi[sl][:], xb[sl][:], AF.Identity,
                                                            bias=st("nmui", sl),
                                                            scale=st("inv", sl)),
                    waits=[("V", v_nmui), xhi_free[sl]], inc="S")
            s_hi0 = sch.val("S")
            if main:
                sch.add("vector", lambda e, sl=sl: e.tensor_copy(xhi32[:], xhi[sl][:]),
                        waits=[("S", s_hi0)], inc="V")
                sch.add("vector", lambda e, sl=sl: e.scalar_tensor_tensor(
                    xlo32[:], xb[sl][:], st("inv", sl), xhi32[:], ALU.mult, ALU.subtract),
                    inc="V")
                sch.add("vector", lambda e, sl=sl: e.tensor_scalar(
                    xlo[sl][:], xlo32[:], st("nmui", sl), None, ALU.add),
                    waits=[xlo_free[sl]], inc="V")
                v_lo = sch.val("V")
                sch.add("vector", lambda e, sl=sl: e.tensor_copy(st("s1", sl), st("s1", sl)),
                        inc="V")
                v_echo = sch.val("V")
                x_free[sl] = ("V", v_echo)
                stat_wfree[sl] = ("V", v_echo)
            else:
                v_lo = 0
                x_free[sl] = ("S", s_hi0)
                stat_wfree[sl] = ("S", s_hi0)
            wtr = [("S", s_hi0)] + [ps_hi_free[0]]
            for c in range(NCH):
                def mmh(e, c=c, sl=sl):
                    return e.matmul(P01[:, c * 128:(c + 1) * 128],
                                    xhi[sl][:, c * 128:(c + 1) * 128], idt[:],
                                    start=True, stop=True)
                sch.add("tensor", mmh, waits=wtr if c == 0 else [], inc="T")
            t_hi = sch.val("T")
            xhi_free[sl] = ("T", t_hi)
            xnfm3 = xnfm[:].rearrange("p (c m) -> p c m", m=TOK)
            def evc(e, t=t, xnfm3=xnfm3):
                return e.activation(xnfm3[:, :, t * 128:(t + 1) * 128],
                                    P01[:].rearrange("p (c m) -> p c m", m=128), AF.Copy)
            sch.add("scalar", evc, waits=[("T", t_hi)], inc="S")
            evA_hi[t] = ("S", sch.val("S"))
            ps_hi_free[0] = evA_hi[t]
            lo_state[t] = (sl, v_lo)

        def a_lo(t):
            sl, v_lo = lo_state.pop(t)
            main = t >= NWARM
            tm = t - NWARM
            if main:
                for half in range(2):
                    wlo = [("V", v_lo), ps_lo_free[0]]
                    for c in range(4):
                        cc = half * 4 + c
                        def mml(e, cc=cc, sl=sl):
                            return e.matmul(P2[:, (cc % 4) * 128:((cc % 4) + 1) * 128],
                                            xlo[sl][:, cc * 128:(cc + 1) * 128], idt[:],
                                            start=True, stop=True)
                        sch.add("tensor", mml, waits=wlo if c == 0 else [], inc="T")
                    t_lo = sch.val("T")
                    w2 = [("T", t_lo)]
                    if half == 0:
                        w2.append(xfl_free[0])
                    sch.add("scalar", lambda e, half=half: e.activation(
                        xfl[0][:, half * 512:(half + 1) * 512], P2[:], AF.Copy),
                        waits=w2, inc="S")
                    ps_lo_free[0] = ("S", sch.val("S"))
                xlo_free[sl] = ("T", t_lo)
                wq = [evA_hi[t], ps_lo_free[0], ("V", V_WG), q_free[0]]
                for c in range(NCH):
                    first = c == 0
                    def mmq1(e, c=c, t=t, first=first, last=(c == NCH - 1)):
                        return e.matmul(P3[:, 0:2 * E],
                                        xnfm[:, c * TOK + t * 128: c * TOK + (t + 1) * 128],
                                        wghl[:, c * 2 * E:(c + 1) * 2 * E],
                                        start=first, stop=last)
                    sch.add("tensor", mmq1, waits=wq if first else [], inc="T")
                for c in range(NCH):
                    def mmq3(e, c=c, first=(c == 0), last=(c == NCH - 1)):
                        return e.matmul(P3[:, 2 * E:3 * E], xfl[0][:, c * 128:(c + 1) * 128],
                                        wghl[:, c * 2 * E: c * 2 * E + E],
                                        start=first, stop=last)
                    sch.add("tensor", mmq3, inc="T")
                t_q = sch.val("T")
                xfl_free[0] = ("T", t_q)
                sch.add("vector", lambda e, tm=tm: e.tensor_copy(
                    qall[:, tm * 3 * E:(tm + 1) * 3 * E], P3[:, 0:3 * E]),
                    waits=[("T", t_q)], inc="V")
                q_free[0] = ("V", sch.val("V"))
            return

        d1_free = [("V", 0), ("V", 0)]
        ssb_free = [("S", 0), ("S", 0)]
        ps_h_free = [("S", 0), ("S", 0)]
        bu = [0]
        yev_blk = {}
        yw_free = {}
        last_scan = [0]

        def stage_b(bi, c):
            off, ln = BLKS[bi]
            u = bu[0]
            bu[0] += 1
            hsl = u % 2
            dsl = u % 2
            PH = P4 if hsl == 0 else P5
            tiles_needed = (off + ln - 1) // 128
            wh = [evA_hi[tiles_needed], ps_h_free[hsl]]
            if bi == 0 and c == 0:
                wh.append(("S", S_WIN))
            for dc in range(NCH):
                first = dc == 0
                last = (dc == NCH - 1) and bi != 0
                def mmh2(e, c=c, dc=dc, off=off, ln=ln, PH=PH, first=first, last=last):
                    return e.matmul(PH[:, 0:ln],
                                    winT[:, dc * D + c * 128: dc * D + (c + 1) * 128],
                                    xnfm[:, dc * TOK + off: dc * TOK + off + ln],
                                    start=first, stop=last)
                sch.add("tensor", mmh2, waits=wh if first else [], inc="T")
            if bi == 0:
                def mmb(e, c=c, off=off, ln=ln, PH=PH):
                    return e.matmul(PH[:, 0:ln], binb[0:1, c * 128:(c + 1) * 128],
                                    brw[0:1, off:off + ln], start=False, stop=True)
                sch.add("tensor", mmb, inc="T")
            t_h = sch.val("T")
            def evd(e, c=c, ln=ln, PH=PH, dsl=dsl, bi=bi):
                if bi == 0:
                    return e.activation(d1[dsl][:, 0:ln], PH[:, 0:ln],
                                        AF.Identity, scale=bpc[:, c:c + 1])
                return e.activation(d1[dsl][:, 0:ln], PH[:, 0:ln],
                                    AF.Identity, scale=bpc[:, c:c + 1],
                                    bias=binB[:, c:c + 1])
            sch.add("scalar", evd, waits=[("T", t_h), d1_free[dsl]], inc="S")
            ps_h_free[hsl] = ("S", sch.val("S"))
            s_d1 = sch.val("S")
            ssl = u % 2
            def scn(e, c=c, ln=ln, dsl=dsl, bi=bi, ssl=ssl):
                init = 0.0 if bi == 0 else stcol[:, c:c + 1]
                return e.tensor_tensor_scan(ssb[ssl][:, 0:ln],
                                            dcol[:, c:c + 1].broadcast_to([128, ln]),
                                            d1[dsl][:, 0:ln], init, ALU.mult, ALU.add)
            sch.add("vector", scn, waits=[("S", s_d1), ssb_free[ssl], ("V", V_ST0)], inc="V")
            v_scan = sch.val("V")
            last_scan[0] = v_scan
            d1_free[dsl] = ("V", v_scan)
            sch.add("vector", lambda e, c=c, ln=ln, ssl=ssl: e.tensor_copy(
                stcol[:, c:c + 1], ssb[ssl][:, ln - 1:ln]), inc="V")
            v_st = sch.val("V")
            mstart = max(off, WARM)
            if off + ln > WARM:
                wv = [("V", v_st)]
                if bi >= YW and c == 0:
                    wv.append(yw_free.get(bi % YW, ("T", 0)))
                def evy(e, c=c, off=off, ln=ln, bi=bi, mstart=mstart, ssl=ssl):
                    return e.activation(
                        yw[:, (bi % YW) * NCH * 512 + c * 512 + (mstart - off):
                           (bi % YW) * NCH * 512 + c * 512 + ln],
                        ssb[ssl][:, mstart - off:ln],
                        AF.Identity, scale=cpc[:, c:c + 1])
                sch.add("scalar", evy, waits=wv, inc="S")
            ssb_free[ssl] = ("S", sch.val("S"))
            if c == NCH - 1:
                yev_blk[bi] = sch.val("S")

        ps_o_free = [("S", 0)]
        c1_done = {}
        prev_c1v = [0]
        outb_free = {}

        def stage_c1(tm):
            qi = tm
            bi = (WARM + tm * 128) // 512
            ywoff = (bi % YW) * NCH * 512
            inoff = WARM + tm * 128 - bi * 512
            w0 = [("S", yev_blk[bi]), ps_o_free[0]]
            if tm == 0:
                w0.append(("S", S_WPREP))
            for half in range(2):
                for ec in range(NCH):
                    first = ec == 0
                    def mmo(e, tm=tm, ec=ec, half=half, ywoff=ywoff, inoff=inoff, first=first):
                        return e.matmul(P67[:, half * 512:(half + 1) * 512],
                                        yw[:, ywoff + ec * 512 + inoff:
                                           ywoff + ec * 512 + inoff + 128],
                                        woutT[:, ec * D + half * 512: ec * D + half * 512 + 512],
                                        start=first, stop=False)
                    sch.add("tensor", mmo, waits=w0 if (first and half == 0) else [], inc="T")
                def mmob(e, half=half):
                    return e.matmul(P67[:, half * 512:(half + 1) * 512], onrb[:],
                                    boutb[0:1, half * 512: half * 512 + 512],
                                    start=False, stop=True)
                sch.add("tensor", mmob, inc="T")
            t_out = sch.val("T")
            yw_free[bi % YW] = ("T", t_out)
            wev = [("T", t_out)]
            sch.add("scalar", lambda e, qi=qi: e.activation(
                outb[:, qi * D:(qi + 1) * D], P67[:], AF.Copy), waits=wev, inc="S")
            s_ev = sch.val("S")
            ps_o_free[0] = ("S", s_ev)
            sch.add("scalar", lambda e, qi=qi: e.activation(
                sq[:], outb[:, qi * D:(qi + 1) * D], AF.Square, accum_out=stat[:, 17:18]),
                waits=[("V", prev_c1v[0])], inc="S")
            s_sq = sch.val("S")
            sch.add("vector", lambda e, qi=qi: e.reduce_sum(
                stat[:, 16:17], outb[:, qi * D:(qi + 1) * D], axis=AX.X),
                waits=[("S", s_ev)], inc="V")
            v_red = sch.val("V")
            sch.add("vector", lambda e, tm=tm: e.tensor_scalar(
                mu16[:, tm:tm + 1], stat[:, 16:17], 1.0 / D, None, ALU.mult),
                waits=[("V", v_red), ("S", s_sq)], inc="V")
            sch.add("vector", lambda e: e.tensor_scalar(stat[:, 18:19], stat[:, 17:18],
                                                        1.0 / D, EPS, ALU.mult, ALU.add), inc="V")
            sch.add("vector", lambda e, tm=tm: e.tensor_mul(stat[:, 19:20], mu16[:, tm:tm + 1],
                                                            mu16[:, tm:tm + 1]), inc="V")
            sch.add("vector", lambda e, tm=tm: e.tensor_sub(var16[:, tm:tm + 1], stat[:, 18:19],
                                                            stat[:, 19:20]), inc="V")
            prev_c1v[0] = sch.val("V")
            c1_done[tm] = prev_c1v[0]

        inv_done = {}

        def stage_cmid(g):
            v_allvar = c1_done[2 * g + 1]
            gs = slice(2 * g, 2 * g + 2)
            sch.add("scalar", lambda e, gs=gs: e.activation(sd16[:, gs], var16[:, gs], AF.Ln),
                    waits=[("V", v_allvar)], inc="S")
            sch.add("scalar", lambda e, gs=gs: e.activation(inv16[:, gs], sd16[:, gs], AF.Exp,
                                                            scale=-0.5), inc="S")
            s_i4 = sch.val("S")
            sch.add("vector", lambda e, gs=gs: e.tensor_scalar(nmi16[:, gs], mu16[:, gs],
                                                               -1.0, None, ALU.mult),
                    waits=[("S", s_i4)], inc="V")
            sch.add("vector", lambda e, gs=gs: e.tensor_mul(nmi16[:, gs], nmi16[:, gs],
                                                            inv16[:, gs]), inc="V")
            inv_done[g] = sch.val("V")

        xre_free = [None, None]
        s_free = [("S", 0)] * 2
        g_free = [("V", 0)] * 2
        ps_z_free = [("S", 0)]
        rb_free = [("ST", 0), ("ST", 0)]
        n_store = [0]
        VKG = [0]
        SAEND = [0]
        WLWE = [0]

        def stage_c2(tm):
            sl = tm % 2
            qi = tm
            g = tm // 2
            xfs = f"XF{sl}"
            wx = [xre_free[sl]] if xre_free[sl] else [("V", VKG[0])]
            sch.add("sync", lambda e, tm=tm, sl=sl: e.dma_start(
                out=xre[sl][:], in_=xs[WARM + tm * 128: WARM + (tm + 1) * 128, :]),
                waits=wx, inc=xfs)
            xf_t = sch.val(xfs)
            sch.add("scalar", lambda e, qi=qi, sl=sl, tm=tm: e.activation(
                sbs[sl][:], outb[:, qi * D:(qi + 1) * D], AF.Identity,
                bias=nmi16[:, tm:tm + 1], scale=inv16[:, tm:tm + 1]),
                waits=[("V", inv_done[g]), ("V", VKG[0]), s_free[sl]], inc="S")
            s_app = sch.val("S")
            sch.add("vector", lambda e, sl=sl: e.tensor_tensor(sbs[sl][:], sbs[sl][:], gbc[:],
                                                               ALU.mult),
                    waits=[("S", s_app)], inc="V")
            sch.add("vector", lambda e, sl=sl: e.tensor_tensor(sbs[sl][:], sbs[sl][:], bbc[:],
                                                               ALU.add), inc="V")
            v_s = sch.val("V")
            for half in range(2):
                for dc in range(NCH):
                    first = dc == 0
                    w = []
                    if first and half == 0:
                        w = [ps_z_free[0], ("WL", WLWE[0])]
                        if tm == 0:
                            w.append(evA_hi[NT - 1])
                    def mmz(e, tm=tm, dc=dc, half=half, first=first):
                        return e.matmul(P01[:, half * 512:(half + 1) * 512],
                                        xnfm[:, dc * TOK + WARM + tm * 128:
                                             dc * TOK + WARM + (tm + 1) * 128],
                                        wesb[:, dc * D + half * 512: dc * D + half * 512 + 512],
                                        start=first, stop=False)
                    sch.add("tensor", mmz, waits=w, inc="T")
                def mmzb(e, half=half):
                    return e.matmul(P01[:, half * 512:(half + 1) * 512], onrb[:],
                                    beb[0:1, half * 512: half * 512 + 512],
                                    start=False, stop=True)
                sch.add("tensor", mmzb, inc="T")
            t_z = sch.val("T")
            sch.add("scalar", lambda e, tm=tm, sl=sl: e.activation(
                gbuf[sl][:], P01[:], AF.Silu, scale=kg16[:, tm:tm + 1]),
                waits=[("T", t_z), g_free[sl]], inc="S")
            s_g = sch.val("S")
            ps_z_free[0] = ("S", s_g)
            sch.add("vector", lambda e, sl=sl: e.tensor_tensor(gbuf[sl][:], sbs[sl][:],
                                                               gbuf[sl][:], ALU.mult),
                    waits=[("S", s_g), ("V", v_s)], inc="V")
            sch.add("vector", lambda e, sl=sl: e.tensor_tensor(
                sbs[sl][:], gbuf[sl][:], xre[sl][:], ALU.add),
                waits=[(xfs, xf_t), rb_free[sl]], inc="V")
            v_res = sch.val("V")
            xre_free[sl] = ("V", v_res)
            g_free[sl] = ("V", v_res)
            sch.add("sync", lambda e, tm=tm, sl=sl: e.dma_start(
                out=y_out[tm * 128:(tm + 1) * 128, :], in_=sbs[sl][:]),
                waits=[("V", v_res)], inc="ST")
            n_store[0] += 16
            rb_free[sl] = ("ST", sch.val("ST"))
            s_free[sl] = ("ST", sch.val("ST"))

        a_front(0)
        a_front(1)
        a_mid(0)
        next_f = [2]
        next_m = [1]
        next_a = [0]

        def maybe_a():
            if next_a[0] < NT:
                a_lo(next_a[0])
                next_a[0] += 1
                if next_m[0] < NT:
                    a_mid(next_m[0])
                    next_m[0] += 1
                if next_f[0] < NT:
                    a_front(next_f[0])
                    next_f[0] += 1

        blk_done = [-1]
        done_c1 = [0]
        cmid_done = [0]

        def maybe_c1():
            tm = done_c1[0]
            if tm >= NMAIN or (WARM + tm * 128) // 512 > blk_done[0]:
                return
            stage_c1(tm)
            if tm % 2 == 1:
                stage_cmid(tm // 2)
                cmid_done[0] += 1
            done_c1[0] += 1

        c2_next = [0]

        def maybe_c2():
            if not R[1]:
                return
            if c2_next[0] >= NMAIN or c2_next[0] >= 2 * cmid_done[0]:
                return
            stage_c2(c2_next[0])
            c2_next[0] += 1

        def router_part1():
            T_QDONE[0] = sch.val("T")
            qa3 = qall[:].rearrange("p (g j) -> p g j", j=3 * E)
            q3 = q16[:].rearrange("p (g j) -> p g j", j=E)
            m3 = m16[:].rearrange("p (g j) -> p g j", j=E)
            e3 = ex16[:].rearrange("p (g j) -> p g j", j=E)
            nx3 = nmx16[:].rearrange("p (g j) -> p g j", j=1)
            sch.add("vector", lambda e: e.scalar_tensor_tensor(
                q3, qa3[:, :, 0:E], 1.0, qa3[:, :, E:2 * E], ALU.mult, ALU.add), inc="V")
            sch.add("vector", lambda e: e.tensor_tensor(
                q3, q3, qa3[:, :, 2 * E:3 * E], ALU.add), inc="V")
            sch.add("vector", lambda e: e.tensor_reduce(nx3, q3, AX.X, ALU.max), inc="V")
            sch.add("vector", lambda e: e.tensor_tensor(
                m3, q3, nx3.broadcast_to([128, NMAIN, E]), ALU.is_equal), inc="V")
            sch.add("vector", lambda e: e.tensor_tensor(
                q3, q3, nx3.broadcast_to([128, NMAIN, E]), ALU.subtract), inc="V")
            v_qn = sch.val("V")
            sch.add("scalar", lambda e: e.activation(ex16[:], q16[:], AF.Exp),
                    waits=[("V", v_qn)], inc="S")
            s_ex = sch.val("S")
            sch.add("vector", lambda e: e.tensor_reduce(
                es16[:].rearrange("p (g j) -> p g j", j=1), e3, AX.X, ALU.add),
                waits=[("S", s_ex)], inc="V")
            sch.add("vector", lambda e: e.reciprocal(gate16[:], es16[:]), inc="V")
            sch.add("vector", lambda e: e.tensor_copy(mball[:], m16[:]), inc="V")
            v_mb = sch.val("V")
            sch.add("tensor", lambda e: e.matmul(P3[0:1, 256:256 + E * NMAIN], onc[:], mball[:],
                                                 start=True, stop=True),
                    waits=[("V", v_mb)], inc="T")
            t_tot = sch.val("T")
            sch.add("vector", lambda e: e.tensor_copy(snapc[:, 0:E], carry[:]),
                    waits=[("T", t_tot)], inc="V")
            sch.add("vector", lambda e: e.tensor_copy(carry[:], P3[0:1, 256:256 + E]), inc="V")
            for tm in range(1, NMAIN):
                sch.add("vector", lambda e, tm=tm: e.tensor_copy(
                    snapc[:, tm * E:(tm + 1) * E], carry[:]), inc="V")
                sch.add("vector", lambda e, tm=tm: e.tensor_tensor(
                    carry[:], carry[:], P3[0:1, 256 + tm * E:256 + (tm + 1) * E], ALU.add),
                    inc="V")
            V_COUNT = sch.val("V")
            dma("gpsimd", wesb[:].rearrange("p (c f) -> p c f", f=D),
                w_e.rearrange("(c p) f -> p c f", p=128), "WL",
                waits=[("T", T_QDONE[0]), ("V", V_COUNT), ("WL", WL_WOUT)])
            WLWE[0] = sch.val("WL")
            dma("gpsimd", cc_in.ap(), carry[:], "WL", waits=[("V", V_COUNT)])
            wl_cc = sch.val("WL")

            def coll(e):
                return e.collective_compute("AllGather", ALU.bypass,
                                            replica_groups=[list(range(NC))],
                                            ins=[cc_in.ap().opt()], outs=[cc_out.ap().opt()])
            sch.add("gpsimd", coll, waits=[("WL", wl_cc)], inc="CC")
            dma("gpsimd", gth[:], cc_out.ap(), "WL", waits=[("CC", sch.val("CC"))])
            WLGTH[0] = sch.val("WL")
            for tm in range(NMAIN):
                def mmc(e, tm=tm):
                    return e.matmul(P3[:, 128 + tm * E:128 + (tm + 1) * E], slo[:],
                                    mball[:, tm * E:(tm + 1) * E], start=True, stop=True)
                sch.add("tensor", mmc, inc="T")
            t_cum = sch.val("T")
            for tm in range(NMAIN):
                sch.add("vector", lambda e, tm=tm: e.scalar_tensor_tensor(
                    cnts[:], m16[:, tm * E:(tm + 1) * E], 1.0,
                    P3[:, 128 + tm * E:128 + (tm + 1) * E],
                    ALU.mult, ALU.mult),
                    waits=[("T", t_cum)] if tm == 0 else [], inc="V")
                sch.add("vector", lambda e, tm=tm: e.reduce_sum(
                    oc16[:, tm:tm + 1], cnts[:], axis=AX.X), inc="V")
                v_oc[0] = sch.val("V")

        def router_part2():
            sch.add("tensor", lambda e: e.matmul(P3[0:1, 256:256 + E], sel[:], gth[:],
                                                 start=True, stop=True),
                    waits=[("WL", WLGTH[0])], inc="T")
            t_pre = sch.val("T")
            sch.add("vector", lambda e: e.tensor_copy(prow[:], P3[0:1, 256:256 + E]),
                    waits=[("T", t_pre)], inc="V")
            v_prow = sch.val("V")
            for tm in range(NMAIN):
                sch.add("vector", lambda e, tm=tm: e.tensor_tensor(
                    cprow[:, tm * E:(tm + 1) * E], snapc[:, tm * E:(tm + 1) * E],
                    prow[:], ALU.add),
                    waits=[("V", v_prow)] if tm == 0 else [], inc="V")
            v_cpr = sch.val("V")
            sch.add("tensor", lambda e: e.matmul(P3[:, 128:256], onr32[:], cprow[:],
                                                 start=True, stop=True),
                    waits=[("V", v_cpr)], inc="T")
            t_bc = sch.val("T")
            for tm in range(NMAIN):
                sch.add("vector", lambda e, tm=tm: e.scalar_tensor_tensor(
                    cnts[:], m16[:, tm * E:(tm + 1) * E], 1.0,
                    P3[:, 128 + tm * E:128 + (tm + 1) * E], ALU.mult, ALU.mult),
                    waits=[("T", t_bc)] if tm == 0 else [], inc="V")
                sch.add("vector", lambda e, tm=tm: e.reduce_sum(
                    op16[:, tm:tm + 1], cnts[:], axis=AX.X), inc="V")
            sch.add("vector", lambda e: e.tensor_tensor(sum16[:], oc16[:], op16[:], ALU.add),
                    inc="V")
            sch.add("vector", lambda e: e.tensor_scalar(keep16[:], sum16[:], CAP, None,
                                                        ALU.is_lt), inc="V")
            sch.add("vector", lambda e: e.tensor_tensor(kg16[:], keep16[:], gate16[:],
                                                        ALU.mult), inc="V")
            VKG[0] = sch.val("V")

        R = [False, False]
        T_QDONE = [0]
        WLGTH = [0]
        v_oc = [0]
        chunks = [(bi, c) for bi in range(5) for c in range(NCH)]
        ci = [0]

        def maybe_b():
            if ci[0] < len(chunks):
                bi, c = chunks[ci[0]]
                off, ln = BLKS[bi]
                if (off + ln - 1) // 128 < next_m[0]:
                    stage_b(bi, c)
                    ci[0] += 1
                    if c == NCH - 1:
                        blk_done[0] = bi
                    return True
            return False

        tog = [0]
        while next_a[0] < NT:
            maybe_b()
            maybe_a()
        router_part1()
        R[0] = True
        while ci[0] < len(chunks):
            maybe_b()
            if not R[1] and blk_done[0] >= 2:
                router_part2()
                R[1] = True
            maybe_c1()
            if tog[0] % 2 == 1 and R[1]:
                maybe_c2()
            tog[0] += 1
        if not R[1]:
            router_part2()
            R[1] = True
        while done_c1[0] < NMAIN or c2_next[0] < NMAIN:
            maybe_c1()
            maybe_c2()

        sch.add("sync", lambda e: e.nop(), waits=[("ST", n_store[0])])

        sems = {}
        for s_name in sch.sem_names():
            sems[s_name] = es.enter_context(nc.semaphore(s_name.replace(".", "_")))
        block = es.enter_context(nc.Block())
        block.sync(lambda e: sch.emit_body(nc, sems, "sync"))
        block.gpsimd(lambda e: sch.emit_body(nc, sems, "gpsimd"))
        block.scalar(lambda e: sch.emit_body(nc, sems, "scalar"))
        block.vector(lambda e: sch.emit_body(nc, sems, "vector"))
        block.tensor(lambda e: sch.emit_body(nc, sems, "tensor"))

    return nc


_NC_CACHE = None


def kernel(L0_attn_x, L0_attn_W_in, L0_attn_b_in, L0_attn_log_A, L0_attn_B_p,
           L0_attn_C_p, L0_attn_W_out, L0_attn_b_out, L0_attn_gamma, L0_attn_beta,
           L0_ffn_Wg, L0_ffn_We, L0_ffn_be):
    global _NC_CACHE
    x = np.ascontiguousarray(np.asarray(L0_attn_x, np.float32).reshape(16384, D))

    if _NC_CACHE is None:
        _NC_CACHE = build_nc()
    nc = _NC_CACHE

    common = {
        "w_in": np.ascontiguousarray(np.asarray(L0_attn_W_in, np.float32)),
        "w_out": np.ascontiguousarray(np.asarray(L0_attn_W_out, np.float32)),
        "w_e": np.ascontiguousarray(np.asarray(L0_ffn_We, np.float32)),
        "w_g": np.ascontiguousarray(np.asarray(L0_ffn_Wg, np.float32)),
        "b_in": np.asarray(L0_attn_b_in, np.float32).reshape(1, D),
        "b_in_c": np.ascontiguousarray(np.asarray(L0_attn_b_in, np.float32).reshape(E, 128).T),
        "b_out": np.asarray(L0_attn_b_out, np.float32).reshape(1, D),
        "b_e": np.asarray(L0_ffn_be, np.float32).reshape(1, D),
        "log_a": np.ascontiguousarray(np.asarray(L0_attn_log_A, np.float32).reshape(E, 128).T),
        "b_p": np.ascontiguousarray(np.asarray(L0_attn_B_p, np.float32).reshape(E, 128).T),
        "c_p": np.ascontiguousarray(np.asarray(L0_attn_C_p, np.float32).reshape(E, 128).T),
        "gam_bc": np.ascontiguousarray(np.broadcast_to(
            np.asarray(L0_attn_gamma, np.float32).reshape(1, D), (128, D))),
        "bet_bc": np.ascontiguousarray(np.broadcast_to(
            np.asarray(L0_attn_beta, np.float32).reshape(1, D), (128, D))),
        "identb": np.eye(128, dtype=np.float32).astype(BF),
        "slob": np.triu(np.ones((128, 128), np.float32), 1).astype(BF),
        "onescol_b": np.ones((128, 1), np.float32).astype(BF),
        "onesrow_b": np.ones((1, 128), np.float32).astype(BF),
        "onesrow32": np.ones((1, 128), np.float32),
    }
    in_maps = []
    for k in range(NC):
        start = k * SLICE
        xsl = np.zeros((TOK, D), np.float32)
        lo = max(0, start - WARM)
        xsl[WARM - (start - lo):] = x[lo:start + SLICE]
        brow = np.ones((1, TOK), np.float32)
        if k == 0:
            brow[0, :WARM] = 0.0
        selv = np.zeros((8, 1), np.float32)
        selv[:k, 0] = 1.0
        im = dict(common)
        im["xs"] = xsl
        im["browb"] = brow.astype(BF)
        im["sel8"] = selv
        in_maps.append(im)

    res = run_bass_kernel_spmd(nc, in_maps, core_ids=list(range(NC)))
    out = np.concatenate([res.results[k]["y_out"] for k in range(NC)], axis=0)
    return np.ascontiguousarray(out).reshape(1, 16384, D)


# version 25
# speedup vs baseline: 1.0531x; 1.0531x over previous
import numpy as np
import ml_dtypes
import concourse.bass as bass
import concourse.mybir as mybir
from concourse.bass_utils import run_bass_kernel_spmd
from contextlib import ExitStack

F32 = mybir.dt.float32
BF16 = mybir.dt.bfloat16
AF = mybir.ActivationFunctionType
ALU = mybir.AluOpType
AX = mybir.AxisListType
BF = ml_dtypes.bfloat16

D = 1024
NC = 8
SLICE = 2048
WARM = 256
TOK = SLICE + WARM
NT = TOK // 128
NWARM = WARM // 128
NMAIN = SLICE // 128
NCH = D // 128
CAP = 2048.0
EPS = 1e-5
E = 8
BLKS = [(0, 512), (512, 512), (1024, 512), (1536, 512), (2048, 256)]
YW = 3


def _incby(sem):
    return 16 if sem[:2] in ("LD", "WL", "XF", "ST", "XP") else 1


class Sched:

    def __init__(self):
        self.ops = []
        self.cnt = {}
        self.wait_claims = {}

    def val(self, sem):
        return self.cnt.get(sem, 0)

    def add(self, eng, fn, waits=(), inc=None):
        new_cnt = None
        if inc is not None:
            new_cnt = self.cnt.get(inc, 0) + _incby(inc)
        rep = (inc, new_cnt) if inc is not None else None
        fixed = []
        for base, v in waits:
            if v <= 0:
                continue
            guard = 0
            while not (_incby(base) == 16 or base == "CC"):
                claim = self.wait_claims.get((base, v))
                if claim is None:
                    self.wait_claims[(base, v)] = (eng, rep)
                    break
                if claim[0] == eng:
                    break
                if claim[1] is None:
                    break
                base, v = claim[1]
                guard += 1
                assert guard < 64, "wait chaining loop"
            fixed.append((base, v))
        if inc is not None:
            self.cnt[inc] = new_cnt
        self.ops.append((eng, fn, tuple(fixed), inc))

    def variant_plan(self):
        plan = {}
        for eng, _, waits, _ in self.ops:
            for sem_name, v in waits:
                if _incby(sem_name) == 16 or sem_name == "CC":
                    key = (sem_name, None)
                else:
                    key = (sem_name, eng)
                plan.setdefault(key, set()).add(v)
        return plan

    def sem_names(self):
        names = {"V.self", "S.self"}
        for (base, eng), _ in self.variant_plan().items():
            names.add(base if eng is None else f"{base}.{eng}")
        return sorted(names)

    def emit_body(self, nc, sems, eng_name):
        eng = {"sync": nc.sync, "gpsimd": nc.gpsimd, "scalar": nc.scalar,
               "vector": nc.vector, "tensor": nc.tensor}[eng_name]
        plan = self.variant_plan()
        variants = {}
        for (base, ceng), vals in plan.items():
            if ceng is not None:
                variants.setdefault(base, []).append((f"{base}.{ceng}", vals))
        self_chain = eng_name in ("vector", "scalar")
        selfsem = {"vector": "V.self", "scalar": "S.self"}.get(eng_name)
        selfval = 0
        cnt = 0
        due = []
        pend = {}
        prev_attach = None
        seen = {}
        for name, fn, waits, inc in self.ops:
            if name != eng_name:
                continue
            wlist = []
            for sem_name, v in waits:
                if not (_incby(sem_name) == 16 or sem_name == "CC"):
                    sem_name = f"{sem_name}.{eng_name}"
                wlist.append((sem_name, v))
            if self_chain and prev_attach is not None:
                wlist.append(prev_attach)
            for sem_name, v in wlist:
                if seen.get(sem_name, 0) < v:
                    eng.wait_ge(sems[sem_name], v)
                    seen[sem_name] = v
            ins = fn(eng)
            if inc is None:
                continue
            step = _incby(inc)
            if step == 16 or inc == "CC":
                ins.then_inc(sems[inc], step)
                continue
            cnt += step
            for vs_name, vals in variants.get(inc, ()):
                pend[vs_name] = pend.get(vs_name, 0) + step
                if cnt in vals and vs_name not in due:
                    due.append(vs_name)
            if due:
                vs = due.pop(0)
                ins.then_inc(sems[vs], pend[vs])
                prev_attach = (vs, cnt)
                pend[vs] = 0
            elif self_chain:
                selfval += 1
                ins.then_inc(sems[selfsem], 1)
                prev_attach = (selfsem, selfval)
        assert not due, f"unflushed sem variants on {eng_name}: {due}"


def build_nc(upto=99):
    nc = bass.Bass("TRN2", target_bir_lowering=False, detect_race_conditions=False)

    xs = nc.dram_tensor("xs", [TOK, D], F32, kind="ExternalInput").ap()
    w_in = nc.dram_tensor("w_in", [D, D], F32, kind="ExternalInput").ap()
    w_out = nc.dram_tensor("w_out", [D, D], F32, kind="ExternalInput").ap()
    w_e = nc.dram_tensor("w_e", [D, D], F32, kind="ExternalInput").ap()
    w_g = nc.dram_tensor("w_g", [D, E], F32, kind="ExternalInput").ap()
    b_in = nc.dram_tensor("b_in", [1, D], F32, kind="ExternalInput").ap()
    b_in_c = nc.dram_tensor("b_in_c", [128, E], F32, kind="ExternalInput").ap()
    b_out = nc.dram_tensor("b_out", [1, D], F32, kind="ExternalInput").ap()
    b_e = nc.dram_tensor("b_e", [1, D], F32, kind="ExternalInput").ap()
    log_a = nc.dram_tensor("log_a", [128, E], F32, kind="ExternalInput").ap()
    b_p = nc.dram_tensor("b_p", [128, E], F32, kind="ExternalInput").ap()
    c_p = nc.dram_tensor("c_p", [128, E], F32, kind="ExternalInput").ap()
    gam_bc = nc.dram_tensor("gam_bc", [128, D], F32, kind="ExternalInput").ap()
    bet_bc = nc.dram_tensor("bet_bc", [128, D], F32, kind="ExternalInput").ap()
    identb = nc.dram_tensor("identb", [128, 128], BF16, kind="ExternalInput").ap()
    slob = nc.dram_tensor("slob", [128, 128], BF16, kind="ExternalInput").ap()
    onescol_b = nc.dram_tensor("onescol_b", [128, 1], BF16, kind="ExternalInput").ap()
    onesrow_b = nc.dram_tensor("onesrow_b", [1, 128], BF16, kind="ExternalInput").ap()
    onesrow32 = nc.dram_tensor("onesrow32", [1, 128], F32, kind="ExternalInput").ap()
    browb = nc.dram_tensor("browb", [1, TOK], BF16, kind="ExternalInput").ap()
    sel8 = nc.dram_tensor("sel8", [8, 1], F32, kind="ExternalInput").ap()
    y_out = nc.dram_tensor("y_out", [SLICE, D], F32, kind="ExternalOutput").ap()
    cc_in = nc.dram_tensor("cc_in", [1, E], F32)
    cc_out = nc.dram_tensor("cc_out", [NC, E], F32)

    es = ExitStack()
    with es:
        def sb(name, shape, dt):
            return nc.alloc_sbuf_tensor(name, shape, dt)

        xnfm = sb("xnfm", [128, NCH * TOK], BF16)
        winT = sb("winT", [128, NCH * D], BF16)
        woutT = sb("woutT", [128, NCH * D], BF16)
        yw = sb("yw", [128, YW * NCH * 512], BF16)
        gbc = sb("gbc", [128, D], F32)
        bbc = sb("bbc", [128, D], F32)
        wg32 = sb("wg32", [128, NCH * E], F32)
        wghl = sb("wghl", [128, NCH * 2 * E], BF16)
        wghf = sb("wghf", [128, NCH * E], F32)
        wgl32 = sb("wgl32", [128, NCH * E], F32)
        qtmp = sb("qtmp", [128, 3 * E], F32)
        qall = sb("qall", [128, 3 * E * NMAIN], F32)
        mball = sb("mball", [128, E * NMAIN], BF16)
        idt = sb("idt", [128, 128], BF16)
        slo = sb("slo", [128, 128], BF16)
        onc = sb("onc", [128, 1], BF16)
        onrb = sb("onrb", [1, 128], BF16)
        onr32 = sb("onr32", [1, 128], F32)
        brw = sb("brw", [1, TOK], BF16)
        sel = sb("sel", [8, 1], F32)
        binb = sb("binb", [1, D], BF16)
        boutb = sb("boutb", [1, D], BF16)
        beb = sb("beb", [1, D], BF16)
        la = sb("la", [128, E], F32)
        bpc = sb("bpc", [128, E], F32)
        cpc = sb("cpc", [128, E], F32)
        spc = sb("spc", [128, E], F32)
        dcol = sb("dcol", [128, E], F32)
        stcol = sb("stcol", [128, E], F32)
        binc = sb("binc", [128, E], F32)
        binB = sb("binB", [128, E], F32)
        m16 = sb("m16", [128, E * NMAIN], F32)
        q16 = sb("q16", [128, E * NMAIN], F32)
        ex16 = sb("ex16", [128, E * NMAIN], F32)
        nmx16 = sb("nmx16", [128, NMAIN], F32)
        oc16 = sb("oc16", [128, NMAIN], F32)
        op16 = sb("op16", [128, NMAIN], F32)
        gate16 = sb("gate16", [128, NMAIN], F32)
        es16 = sb("es16", [128, NMAIN], F32)
        kg16 = sb("kg16", [128, NMAIN], F32)
        sum16 = sb("sum16", [128, NMAIN], F32)
        keep16 = sb("keep16", [128, NMAIN], F32)
        mu16 = sb("mu16", [128, NMAIN], F32)
        var16 = sb("var16", [128, NMAIN], F32)
        sd16 = sb("sd16", [128, NMAIN], F32)
        inv16 = sb("inv16", [128, NMAIN], F32)
        nmi16 = sb("nmi16", [128, NMAIN], F32)
        carry = sb("carry", [1, E], F32)
        snapc = sb("snapc", [1, E * NMAIN], F32)
        prow = sb("prow", [1, E], F32)
        cprow = sb("cprow", [1, E * NMAIN], F32)
        gth = sb("gth", [8, E], F32)
        mbuf = sb("mbuf", [128, E], BF16)
        cnts = sb("cnts", [128, E], F32)
        sq = sb("sq", [128, D], F32)
        stat = sb("stat", [128, 24], F32)

        AREN = 83968
        base = nc.bump_sbuf(AREN)
        assert base is not None, "SBUF arena allocation failed"
        off0 = base[0]

        def at(name, off, shape, dt):
            return nc.alloc_sbuf_tensor_at(name, shape, dt, offset=off0 + off)

        xb = [at("xb0", 0, [128, D], F32), at("xb1", 4096, [128, D], F32)]
        xhi = [at("xhi0", 8192, [128, D], BF16), at("xhi1", 10240, [128, D], BF16)]
        xlo = [at("xlo0", 12288, [128, D], BF16), at("xlo1", 14336, [128, D], BF16)]
        xhi32 = at("xh32", 16384, [128, D], F32)
        xlo32 = at("xl32", 20480, [128, D], F32)
        wesb = at("wesb", 8192, [128, NCH * D], BF16)
        xfl = [at("xfl0", 24576, [128, NCH * 128], BF16)]
        d1 = [at("d1a", 26624, [128, 512], F32), at("d1b", 28672, [128, 512], F32)]
        ssb = [at("ssb0", 30720, [128, 512], F32), at("ssb1", 81920, [128, 512], F32)]
        NOUT = 16
        outb = at("outb", 32768, [128, NOUT * D], BF16)
        sbs = [at("sbs0", 65536, [128, D], F32), at("sbs1", 69632, [128, D], F32)]
        gbuf = [at("gb0", 73728, [128, D], F32), at("gb1", 77824, [128, D], F32)]
        xre = xb
        wst = at("wst", 32768, [128, NCH * D], BF16)

        P01 = nc.alloc_psum_tensor("P01", [128, 1024], F32)
        P2 = nc.alloc_psum_tensor("P2", [128, 512], F32)
        P3 = nc.alloc_psum_tensor("P3", [128, 512], F32)
        P4 = nc.alloc_psum_tensor("P4", [128, 512], F32)
        P5 = nc.alloc_psum_tensor("P5", [128, 512], F32)
        P67 = nc.alloc_psum_tensor("P67", [128, 1024], F32)

        sch = Sched()

        def dma(eng_name, dst, src, sem, waits=()):
            sch.add(eng_name, lambda e, d=dst, s=src: e.dma_start(out=d, in_=s),
                    waits=waits, inc=sem)

        dma("sync", gbc[:], gam_bc, "LDC")
        dma("sync", bbc[:], bet_bc, "LDC")
        dma("sync", idt[:], identb, "LDC")
        dma("sync", slo[:], slob, "LDC")
        dma("sync", onc[:], onescol_b, "LDC")
        dma("sync", onrb[:], onesrow_b, "LDC")
        dma("sync", onr32[:], onesrow32, "LDC")
        dma("sync", brw[:], browb, "LDC")
        dma("sync", sel[:], sel8, "LDC")
        dma("sync", la[:], log_a, "LDC")
        dma("sync", bpc[:], b_p, "LDC")
        dma("sync", cpc[:], c_p, "LDC")
        dma("sync", binc[:], b_in_c, "LDC")
        dma("sync", wg32[:].rearrange("p (c j) -> p c j", j=E),
            w_g.rearrange("(c p) j -> p c j", p=128), "LDC")
        LD_CONST = sch.val("LDC")
        dma("gpsimd", binb[:], b_in, "WL")
        dma("gpsimd", boutb[:], b_out, "WL")
        dma("gpsimd", beb[:], b_e, "WL")
        dma("gpsimd", wst[:].rearrange("p (c d) -> p c d", d=D),
            w_in.rearrange("(c p) d -> p c d", p=128), "WL")
        WL_WIN = sch.val("WL")

        wconst = [("LDC", LD_CONST)]
        sch.add("scalar", lambda e: e.activation(spc[:], la[:], AF.Exp), waits=wconst, inc="S")
        s_e1 = sch.val("S")
        sch.add("vector", lambda e: e.tensor_scalar(spc[:], spc[:], 1.0, None, ALU.add),
                waits=[("S", s_e1)], inc="V")
        v_e1 = sch.val("V")
        sch.add("scalar", lambda e: e.activation(spc[:], spc[:], AF.Ln),
                waits=[("V", v_e1)], inc="S")
        s_ln = sch.val("S")
        sch.add("scalar", lambda e: e.activation(dcol[:], spc[:], AF.Exp, scale=-1.0),
                waits=[("S", s_ln)], inc="S")
        S_DECAY = sch.val("S")
        wghl3 = wghl[:].rearrange("p (c j) -> p c j", j=2 * E)
        wg32_3 = wg32[:].rearrange("p (c j) -> p c j", j=E)
        wghf3 = wghf[:].rearrange("p (c j) -> p c j", j=E)
        wgl32_3 = wgl32[:].rearrange("p (c j) -> p c j", j=E)
        sch.add("vector", lambda e: e.tensor_copy(wghl3[:, :, 0:E], wg32_3), waits=wconst, inc="V")
        v1 = sch.val("V")
        sch.add("vector", lambda e: e.tensor_copy(wghf3, wghl3[:, :, 0:E]),
                waits=[("V", v1)], inc="V")
        v1b = sch.val("V")
        sch.add("vector", lambda e: e.tensor_tensor(wgl32[:], wg32[:], wghf[:], ALU.subtract),
                waits=[("V", v1b)], inc="V")
        sch.add("vector", lambda e: e.tensor_copy(wghl3[:, :, E:2 * E], wgl32_3), inc="V")
        V_WG = sch.val("V")
        sch.add("vector", lambda e: e.memset(stcol[:], 0.0), inc="V")
        sch.add("vector", lambda e: e.memset(carry[:], 0.0), inc="V")
        sch.add("vector", lambda e: e.tensor_mul(binB[:], binc[:], bpc[:]), waits=wconst, inc="V")
        V_ST0 = sch.val("V")

        def w_transpose(dst_sb, wait_first, prev_ev=None):
            ev = {}
            for dc in range(NCH):
                slot = dc % 2
                half_ev = 0
                for er in range(NCH):
                    w = []
                    if er == 0:
                        w = list(wait_first)
                        if dc >= 2:
                            w.append(("S", ev[dc - 2]))
                        elif prev_ev is not None:
                            w.append(("S", prev_ev[6 + slot]))
                    elif er == 4:
                        w = [("S", half_ev)]
                    def mm(e, er=er, dc=dc, slot=slot):
                        return e.matmul(P67[:, slot * 512 + (er % 4) * 128:
                                            slot * 512 + ((er % 4) + 1) * 128],
                                        wst[:, er * D + dc * 128: er * D + (dc + 1) * 128],
                                        idt[:], start=True, stop=True)
                    sch.add("tensor", mm, waits=w, inc="T")
                    if er in (3, 7):
                        t_done = sch.val("T")
                        half = 0 if er == 3 else 1
                        def evc(e, dc=dc, slot=slot, dst_sb=dst_sb, half=half):
                            return e.activation(
                                dst_sb[:, dc * D + half * 512: dc * D + half * 512 + 512],
                                P67[:, slot * 512:(slot + 1) * 512], AF.Copy)
                        sch.add("scalar", evc, waits=[("T", t_done)], inc="S")
                        half_ev = sch.val("S")
                ev[dc] = sch.val("S")
            return ev

        ev_win = w_transpose(winT, [("WL", WL_WIN), ("LDC", LD_CONST)])
        dma("gpsimd", wst[:].rearrange("p (c d) -> p c d", d=D),
            w_out.rearrange("(c p) d -> p c d", p=128), "WL",
            waits=[("S", ev_win[7]), ("WL", WL_WIN)])
        WL_WOUT = sch.val("WL")
        ev_wout = w_transpose(woutT, [("WL", WL_WOUT)], prev_ev=ev_win)
        S_WIN = max(ev_win[6], ev_win[7])
        S_WPREP = max(ev_wout[6], ev_wout[7])

        COL = {k: i for i, k in enumerate(
            ["s1", "mu", "mu2", "var", "sd", "inv", "nmui", "s2"])}

        def st(col, sl):
            return stat[:, sl * 8 + COL[col]:sl * 8 + COL[col] + 1]

        x_free = [("V", 0)] * 2
        xhi_free = [("T", 0)] * 2
        xlo_free = [("T", 0)] * 2
        xfl_free = [("T", 0)]
        evA_hi = {}
        q_free = [("V", 0)]
        ps_hi_free = [("S", 0), ("S", 0)]
        ps_lo_free = [("S", 0)]
        t_tot_prev = [0]

        stat_wfree = [("S", 0), ("S", 0)]
        front_nmui = {}

        def a_front(t):
            sl = t % 2
            ldx = f"LDX{sl}"
            sch.add("sync", lambda e, t=t, sl=sl: e.dma_start(
                out=xb[sl][:], in_=xs[t * 128:(t + 1) * 128, :]),
                waits=[x_free[sl]], inc=ldx)
            wx = [(ldx, sch.val(ldx))]
            sch.add("vector", lambda e, sl=sl: e.reduce_sum(st("s1", sl), xb[sl][:], axis=AX.X),
                    waits=wx, inc="V")
            v_s1 = sch.val("V")
            sch.add("scalar", lambda e, sl=sl: e.activation(sq[:], xb[sl][:], AF.Square,
                                                            accum_out=st("s2", sl)),
                    waits=wx + [stat_wfree[sl]], inc="S")
            s_s2 = sch.val("S")
            sch.add("vector", lambda e, sl=sl: e.tensor_scalar(st("mu", sl), st("s1", sl),
                                                               1.0 / D, None, ALU.mult),
                    waits=[("V", v_s1)], inc="V")
            sch.add("vector", lambda e, sl=sl: e.tensor_mul(st("mu2", sl), st("mu", sl),
                                                            st("mu", sl)), inc="V")
            sch.add("vector", lambda e, sl=sl: e.tensor_scalar(st("var", sl), st("s2", sl),
                                                               1.0 / D, EPS, ALU.mult, ALU.add),
                    waits=[("S", s_s2)], inc="V")
            sch.add("vector", lambda e, sl=sl: e.tensor_sub(st("var", sl), st("var", sl),
                                                            st("mu2", sl)), inc="V")
            v_var = sch.val("V")
            sch.add("scalar", lambda e, sl=sl: e.activation(st("sd", sl), st("var", sl), AF.Ln),
                    waits=[("V", v_var)], inc="S")
            sch.add("scalar", lambda e, sl=sl: e.activation(st("inv", sl), st("sd", sl), AF.Exp,
                                                            scale=-0.5), inc="S")
            s_inv = sch.val("S")
            sch.add("vector", lambda e, sl=sl: e.tensor_scalar(st("nmui", sl), st("mu", sl),
                                                               -1.0, st("inv", sl),
                                                               ALU.mult, ALU.mult),
                    waits=[("S", s_inv)], inc="V")
            front_nmui[t] = sch.val("V")

        lo_state = {}

        def a_mid(t):
            sl = t % 2
            main = t >= NWARM
            tm = t - NWARM
            v_nmui = front_nmui[t]
            sch.add("scalar", lambda e, sl=sl: e.activation(xhi[sl][:], xb[sl][:], AF.Identity,
                                                            bias=st("nmui", sl),
                                                            scale=st("inv", sl)),
                    waits=[("V", v_nmui), xhi_free[sl]], inc="S")
            s_hi0 = sch.val("S")
            if main:
                sch.add("vector", lambda e, sl=sl: e.tensor_copy(xhi32[:], xhi[sl][:]),
                        waits=[("S", s_hi0)], inc="V")
                sch.add("vector", lambda e, sl=sl: e.scalar_tensor_tensor(
                    xlo32[:], xb[sl][:], st("inv", sl), xhi32[:], ALU.mult, ALU.subtract),
                    inc="V")
                sch.add("vector", lambda e, sl=sl: e.tensor_scalar(
                    xlo[sl][:], xlo32[:], st("nmui", sl), None, ALU.add),
                    waits=[xlo_free[sl]], inc="V")
                v_lo = sch.val("V")
                sch.add("vector", lambda e, sl=sl: e.tensor_copy(st("s1", sl), st("s1", sl)),
                        inc="V")
                v_echo = sch.val("V")
                x_free[sl] = ("V", v_echo)
                stat_wfree[sl] = ("V", v_echo)
            else:
                v_lo = 0
                x_free[sl] = ("S", s_hi0)
                stat_wfree[sl] = ("S", s_hi0)
            xnfm3 = xnfm[:].rearrange("p (c m) -> p c m", m=TOK)
            P013 = P01[:].rearrange("p (c m) -> p c m", m=128)
            for half in range(2):
                wtr = [ps_hi_free[half]]
                if half == 0:
                    wtr.append(("S", s_hi0))
                for c in range(4):
                    cc = half * 4 + c
                    def mmh(e, cc=cc, sl=sl):
                        return e.matmul(P01[:, cc * 128:(cc + 1) * 128],
                                        xhi[sl][:, cc * 128:(cc + 1) * 128], idt[:],
                                        start=True, stop=True)
                    sch.add("tensor", mmh, waits=wtr if c == 0 else [], inc="T")
                t_hi = sch.val("T")
                def evc(e, t=t, half=half):
                    return e.activation(
                        xnfm3[:, half * 4:half * 4 + 4, t * 128:(t + 1) * 128],
                        P013[:, half * 4:half * 4 + 4, :], AF.Copy)
                sch.add("scalar", evc, waits=[("T", t_hi)], inc="S")
                ps_hi_free[half] = ("S", sch.val("S"))
            xhi_free[sl] = ("T", t_hi)
            evA_hi[t] = ("S", sch.val("S"))
            lo_state[t] = (sl, v_lo)

        def a_lo(t):
            sl, v_lo = lo_state.pop(t)
            main = t >= NWARM
            tm = t - NWARM
            if main:
                for half in range(2):
                    wlo = [("V", v_lo), ps_lo_free[0]]
                    for c in range(4):
                        cc = half * 4 + c
                        def mml(e, cc=cc, sl=sl):
                            return e.matmul(P2[:, (cc % 4) * 128:((cc % 4) + 1) * 128],
                                            xlo[sl][:, cc * 128:(cc + 1) * 128], idt[:],
                                            start=True, stop=True)
                        sch.add("tensor", mml, waits=wlo if c == 0 else [], inc="T")
                    t_lo = sch.val("T")
                    w2 = [("T", t_lo)]
                    if half == 0:
                        w2.append(xfl_free[0])
                    sch.add("scalar", lambda e, half=half: e.activation(
                        xfl[0][:, half * 512:(half + 1) * 512], P2[:], AF.Copy),
                        waits=w2, inc="S")
                    ps_lo_free[0] = ("S", sch.val("S"))
                xlo_free[sl] = ("T", t_lo)
                wq = [evA_hi[t], ps_lo_free[0], ("V", V_WG), q_free[0]]
                for c in range(NCH):
                    first = c == 0
                    def mmq1(e, c=c, t=t, first=first, last=(c == NCH - 1)):
                        return e.matmul(P3[:, 0:2 * E],
                                        xnfm[:, c * TOK + t * 128: c * TOK + (t + 1) * 128],
                                        wghl[:, c * 2 * E:(c + 1) * 2 * E],
                                        start=first, stop=last)
                    sch.add("tensor", mmq1, waits=wq if first else [], inc="T")
                for c in range(NCH):
                    def mmq3(e, c=c, first=(c == 0), last=(c == NCH - 1)):
                        return e.matmul(P3[:, 2 * E:3 * E], xfl[0][:, c * 128:(c + 1) * 128],
                                        wghl[:, c * 2 * E: c * 2 * E + E],
                                        start=first, stop=last)
                    sch.add("tensor", mmq3, inc="T")
                t_q = sch.val("T")
                xfl_free[0] = ("T", t_q)
                sch.add("vector", lambda e, tm=tm: e.tensor_copy(
                    qall[:, tm * 3 * E:(tm + 1) * 3 * E], P3[:, 0:3 * E]),
                    waits=[("T", t_q)], inc="V")
                q_free[0] = ("V", sch.val("V"))
            return

        d1_free = [("V", 0), ("V", 0)]
        ssb_free = [("S", 0), ("S", 0)]
        ps_h_free = [("S", 0), ("S", 0)]
        bu = [0]
        yev_blk = {}
        yw_free = {}
        last_scan = [0]

        def stage_b(bi, c):
            off, ln = BLKS[bi]
            u = bu[0]
            bu[0] += 1
            hsl = u % 2
            dsl = u % 2
            PH = P4 if hsl == 0 else P5
            tiles_needed = (off + ln - 1) // 128
            wh = [evA_hi[tiles_needed], ps_h_free[hsl]]
            if bi == 0 and c == 0:
                wh.append(("S", S_WIN))
            for dc in range(NCH):
                first = dc == 0
                last = (dc == NCH - 1) and bi != 0
                def mmh2(e, c=c, dc=dc, off=off, ln=ln, PH=PH, first=first, last=last):
                    return e.matmul(PH[:, 0:ln],
                                    winT[:, dc * D + c * 128: dc * D + (c + 1) * 128],
                                    xnfm[:, dc * TOK + off: dc * TOK + off + ln],
                                    start=first, stop=last)
                sch.add("tensor", mmh2, waits=wh if first else [], inc="T")
            if bi == 0:
                def mmb(e, c=c, off=off, ln=ln, PH=PH):
                    return e.matmul(PH[:, 0:ln], binb[0:1, c * 128:(c + 1) * 128],
                                    brw[0:1, off:off + ln], start=False, stop=True)
                sch.add("tensor", mmb, inc="T")
            t_h = sch.val("T")
            def evd(e, c=c, ln=ln, PH=PH, dsl=dsl, bi=bi):
                if bi == 0:
                    return e.activation(d1[dsl][:, 0:ln], PH[:, 0:ln],
                                        AF.Identity, scale=bpc[:, c:c + 1])
                return e.activation(d1[dsl][:, 0:ln], PH[:, 0:ln],
                                    AF.Identity, scale=bpc[:, c:c + 1],
                                    bias=binB[:, c:c + 1])
            sch.add("scalar", evd, waits=[("T", t_h), d1_free[dsl]], inc="S")
            ps_h_free[hsl] = ("S", sch.val("S"))
            s_d1 = sch.val("S")
            ssl = u % 2
            def scn(e, c=c, ln=ln, dsl=dsl, bi=bi, ssl=ssl):
                init = 0.0 if bi == 0 else stcol[:, c:c + 1]
                return e.tensor_tensor_scan(ssb[ssl][:, 0:ln],
                                            dcol[:, c:c + 1].broadcast_to([128, ln]),
                                            d1[dsl][:, 0:ln], init, ALU.mult, ALU.add)
            sch.add("vector", scn, waits=[("S", s_d1), ssb_free[ssl], ("V", V_ST0)], inc="V")
            v_scan = sch.val("V")
            last_scan[0] = v_scan
            d1_free[dsl] = ("V", v_scan)
            sch.add("vector", lambda e, c=c, ln=ln, ssl=ssl: e.tensor_copy(
                stcol[:, c:c + 1], ssb[ssl][:, ln - 1:ln]), inc="V")
            v_st = sch.val("V")
            mstart = max(off, WARM)
            if off + ln > WARM:
                wv = [("V", v_st)]
                if bi >= YW and c == 0:
                    wv.append(yw_free.get(bi % YW, ("T", 0)))
                def evy(e, c=c, off=off, ln=ln, bi=bi, mstart=mstart, ssl=ssl):
                    return e.activation(
                        yw[:, (bi % YW) * NCH * 512 + c * 512 + (mstart - off):
                           (bi % YW) * NCH * 512 + c * 512 + ln],
                        ssb[ssl][:, mstart - off:ln],
                        AF.Identity, scale=cpc[:, c:c + 1])
                sch.add("scalar", evy, waits=wv, inc="S")
            ssb_free[ssl] = ("S", sch.val("S"))
            if c == NCH - 1:
                yev_blk[bi] = sch.val("S")

        ps_o_free = [("S", 0)]
        c1_done = {}
        prev_c1v = [0]
        outb_free = {}

        def stage_c1(tm):
            qi = tm
            bi = (WARM + tm * 128) // 512
            ywoff = (bi % YW) * NCH * 512
            inoff = WARM + tm * 128 - bi * 512
            w0 = [("S", yev_blk[bi]), ps_o_free[0]]
            if tm == 0:
                w0.append(("S", S_WPREP))
            for half in range(2):
                for ec in range(NCH):
                    first = ec == 0
                    def mmo(e, tm=tm, ec=ec, half=half, ywoff=ywoff, inoff=inoff, first=first):
                        return e.matmul(P67[:, half * 512:(half + 1) * 512],
                                        yw[:, ywoff + ec * 512 + inoff:
                                           ywoff + ec * 512 + inoff + 128],
                                        woutT[:, ec * D + half * 512: ec * D + half * 512 + 512],
                                        start=first, stop=False)
                    sch.add("tensor", mmo, waits=w0 if (first and half == 0) else [], inc="T")
                def mmob(e, half=half):
                    return e.matmul(P67[:, half * 512:(half + 1) * 512], onrb[:],
                                    boutb[0:1, half * 512: half * 512 + 512],
                                    start=False, stop=True)
                sch.add("tensor", mmob, inc="T")
            t_out = sch.val("T")
            yw_free[bi % YW] = ("T", t_out)
            wev = [("T", t_out)]
            sch.add("scalar", lambda e, qi=qi: e.activation(
                outb[:, qi * D:(qi + 1) * D], P67[:], AF.Copy), waits=wev, inc="S")
            s_ev = sch.val("S")
            ps_o_free[0] = ("S", s_ev)
            sch.add("scalar", lambda e, qi=qi: e.activation(
                sq[:], outb[:, qi * D:(qi + 1) * D], AF.Square, accum_out=stat[:, 17:18]),
                waits=[("V", prev_c1v[0])], inc="S")
            s_sq = sch.val("S")
            sch.add("vector", lambda e, qi=qi: e.reduce_sum(
                stat[:, 16:17], outb[:, qi * D:(qi + 1) * D], axis=AX.X),
                waits=[("S", s_ev)], inc="V")
            v_red = sch.val("V")
            sch.add("vector", lambda e, tm=tm: e.tensor_scalar(
                mu16[:, tm:tm + 1], stat[:, 16:17], 1.0 / D, None, ALU.mult),
                waits=[("V", v_red), ("S", s_sq)], inc="V")
            sch.add("vector", lambda e: e.tensor_scalar(stat[:, 18:19], stat[:, 17:18],
                                                        1.0 / D, EPS, ALU.mult, ALU.add), inc="V")
            sch.add("vector", lambda e, tm=tm: e.tensor_mul(stat[:, 19:20], mu16[:, tm:tm + 1],
                                                            mu16[:, tm:tm + 1]), inc="V")
            sch.add("vector", lambda e, tm=tm: e.tensor_sub(var16[:, tm:tm + 1], stat[:, 18:19],
                                                            stat[:, 19:20]), inc="V")
            prev_c1v[0] = sch.val("V")
            c1_done[tm] = prev_c1v[0]

        inv_done = {}

        def stage_cmid(g):
            v_allvar = c1_done[2 * g + 1]
            gs = slice(2 * g, 2 * g + 2)
            sch.add("scalar", lambda e, gs=gs: e.activation(sd16[:, gs], var16[:, gs], AF.Ln),
                    waits=[("V", v_allvar)], inc="S")
            sch.add("scalar", lambda e, gs=gs: e.activation(inv16[:, gs], sd16[:, gs], AF.Exp,
                                                            scale=-0.5), inc="S")
            s_i4 = sch.val("S")
            sch.add("vector", lambda e, gs=gs: e.tensor_scalar(nmi16[:, gs], mu16[:, gs],
                                                               -1.0, None, ALU.mult),
                    waits=[("S", s_i4)], inc="V")
            sch.add("vector", lambda e, gs=gs: e.tensor_mul(nmi16[:, gs], nmi16[:, gs],
                                                            inv16[:, gs]), inc="V")
            inv_done[g] = sch.val("V")

        xre_free = [None, None]
        s_free = [("S", 0)] * 2
        g_free = [("V", 0)] * 2
        ps_z_free = [("S", 0)]
        rb_free = [("ST", 0), ("ST", 0)]
        n_store = [0]
        VKG = [0]
        SAEND = [0]
        WLWE = [0]

        def stage_c2(tm):
            sl = tm % 2
            qi = tm
            g = tm // 2
            xfs = f"XF{sl}"
            wx = [xre_free[sl]] if xre_free[sl] else [("V", VKG[0])]
            sch.add("sync", lambda e, tm=tm, sl=sl: e.dma_start(
                out=xre[sl][:], in_=xs[WARM + tm * 128: WARM + (tm + 1) * 128, :]),
                waits=wx, inc=xfs)
            xf_t = sch.val(xfs)
            sch.add("scalar", lambda e, qi=qi, sl=sl, tm=tm: e.activation(
                sbs[sl][:], outb[:, qi * D:(qi + 1) * D], AF.Identity,
                bias=nmi16[:, tm:tm + 1], scale=inv16[:, tm:tm + 1]),
                waits=[("V", inv_done[g]), ("V", VKG[0]), s_free[sl]], inc="S")
            s_app = sch.val("S")
            sch.add("vector", lambda e, sl=sl: e.tensor_tensor(sbs[sl][:], sbs[sl][:], gbc[:],
                                                               ALU.mult),
                    waits=[("S", s_app)], inc="V")
            sch.add("vector", lambda e, sl=sl: e.tensor_tensor(sbs[sl][:], sbs[sl][:], bbc[:],
                                                               ALU.add), inc="V")
            v_s = sch.val("V")
            for half in range(2):
                for dc in range(NCH):
                    first = dc == 0
                    w = []
                    if first and half == 0:
                        w = [ps_z_free[0], ("WL", WLWE[0])]
                        if tm == 0:
                            w.append(evA_hi[NT - 1])
                    def mmz(e, tm=tm, dc=dc, half=half, first=first):
                        return e.matmul(P01[:, half * 512:(half + 1) * 512],
                                        xnfm[:, dc * TOK + WARM + tm * 128:
                                             dc * TOK + WARM + (tm + 1) * 128],
                                        wesb[:, dc * D + half * 512: dc * D + half * 512 + 512],
                                        start=first, stop=False)
                    sch.add("tensor", mmz, waits=w, inc="T")
                def mmzb(e, half=half):
                    return e.matmul(P01[:, half * 512:(half + 1) * 512], onrb[:],
                                    beb[0:1, half * 512: half * 512 + 512],
                                    start=False, stop=True)
                sch.add("tensor", mmzb, inc="T")
            t_z = sch.val("T")
            sch.add("scalar", lambda e, tm=tm, sl=sl: e.activation(
                gbuf[sl][:], P01[:], AF.Silu, scale=kg16[:, tm:tm + 1]),
                waits=[("T", t_z), g_free[sl]], inc="S")
            s_g = sch.val("S")
            ps_z_free[0] = ("S", s_g)
            sch.add("vector", lambda e, sl=sl: e.tensor_tensor(gbuf[sl][:], sbs[sl][:],
                                                               gbuf[sl][:], ALU.mult),
                    waits=[("S", s_g), ("V", v_s)], inc="V")
            sch.add("vector", lambda e, sl=sl: e.tensor_tensor(
                sbs[sl][:], gbuf[sl][:], xre[sl][:], ALU.add),
                waits=[(xfs, xf_t), rb_free[sl]], inc="V")
            v_res = sch.val("V")
            xre_free[sl] = ("V", v_res)
            g_free[sl] = ("V", v_res)
            sch.add("sync", lambda e, tm=tm, sl=sl: e.dma_start(
                out=y_out[tm * 128:(tm + 1) * 128, :], in_=sbs[sl][:]),
                waits=[("V", v_res)], inc="ST")
            n_store[0] += 16
            rb_free[sl] = ("ST", sch.val("ST"))
            s_free[sl] = ("ST", sch.val("ST"))

        a_front(0)
        a_front(1)
        a_mid(0)
        next_f = [2]
        next_m = [1]
        next_a = [0]

        def maybe_a():
            if next_a[0] < NT:
                a_lo(next_a[0])
                next_a[0] += 1
                if next_m[0] < NT:
                    a_mid(next_m[0])
                    next_m[0] += 1
                if next_f[0] < NT:
                    a_front(next_f[0])
                    next_f[0] += 1

        blk_done = [-1]
        done_c1 = [0]
        cmid_done = [0]

        def maybe_c1():
            tm = done_c1[0]
            if tm >= NMAIN or (WARM + tm * 128) // 512 > blk_done[0]:
                return
            stage_c1(tm)
            if tm % 2 == 1:
                stage_cmid(tm // 2)
                cmid_done[0] += 1
            done_c1[0] += 1

        c2_next = [0]

        def maybe_c2():
            if not R[1]:
                return
            if c2_next[0] >= NMAIN or c2_next[0] >= 2 * cmid_done[0]:
                return
            stage_c2(c2_next[0])
            c2_next[0] += 1

        def router_part1():
            T_QDONE[0] = sch.val("T")
            qa3 = qall[:].rearrange("p (g j) -> p g j", j=3 * E)
            q3 = q16[:].rearrange("p (g j) -> p g j", j=E)
            m3 = m16[:].rearrange("p (g j) -> p g j", j=E)
            e3 = ex16[:].rearrange("p (g j) -> p g j", j=E)
            nx3 = nmx16[:].rearrange("p (g j) -> p g j", j=1)
            sch.add("vector", lambda e: e.scalar_tensor_tensor(
                q3, qa3[:, :, 0:E], 1.0, qa3[:, :, E:2 * E], ALU.mult, ALU.add), inc="V")
            sch.add("vector", lambda e: e.tensor_tensor(
                q3, q3, qa3[:, :, 2 * E:3 * E], ALU.add), inc="V")
            sch.add("vector", lambda e: e.tensor_reduce(nx3, q3, AX.X, ALU.max), inc="V")
            sch.add("vector", lambda e: e.tensor_tensor(
                m3, q3, nx3.broadcast_to([128, NMAIN, E]), ALU.is_equal), inc="V")
            sch.add("vector", lambda e: e.tensor_tensor(
                q3, q3, nx3.broadcast_to([128, NMAIN, E]), ALU.subtract), inc="V")
            v_qn = sch.val("V")
            sch.add("scalar", lambda e: e.activation(ex16[:], q16[:], AF.Exp),
                    waits=[("V", v_qn)], inc="S")
            s_ex = sch.val("S")
            sch.add("vector", lambda e: e.tensor_reduce(
                es16[:].rearrange("p (g j) -> p g j", j=1), e3, AX.X, ALU.add),
                waits=[("S", s_ex)], inc="V")
            sch.add("vector", lambda e: e.reciprocal(gate16[:], es16[:]), inc="V")
            sch.add("vector", lambda e: e.tensor_copy(mball[:], m16[:]), inc="V")
            v_mb = sch.val("V")
            sch.add("tensor", lambda e: e.matmul(P3[0:1, 256:256 + E * NMAIN], onc[:], mball[:],
                                                 start=True, stop=True),
                    waits=[("V", v_mb)], inc="T")
            t_tot = sch.val("T")
            sch.add("vector", lambda e: e.tensor_copy(snapc[:, 0:E], carry[:]),
                    waits=[("T", t_tot)], inc="V")
            sch.add("vector", lambda e: e.tensor_copy(carry[:], P3[0:1, 256:256 + E]), inc="V")
            for tm in range(1, NMAIN):
                sch.add("vector", lambda e, tm=tm: e.tensor_copy(
                    snapc[:, tm * E:(tm + 1) * E], carry[:]), inc="V")
                sch.add("vector", lambda e, tm=tm: e.tensor_tensor(
                    carry[:], carry[:], P3[0:1, 256 + tm * E:256 + (tm + 1) * E], ALU.add),
                    inc="V")
            V_COUNT = sch.val("V")
            dma("gpsimd", wesb[:].rearrange("p (c f) -> p c f", f=D),
                w_e.rearrange("(c p) f -> p c f", p=128), "WL",
                waits=[("T", T_QDONE[0]), ("V", V_COUNT), ("WL", WL_WOUT)])
            WLWE[0] = sch.val("WL")
            dma("gpsimd", cc_in.ap(), carry[:], "WL", waits=[("V", V_COUNT)])
            wl_cc = sch.val("WL")

            def coll(e):
                return e.collective_compute("AllGather", ALU.bypass,
                                            replica_groups=[list(range(NC))],
                                            ins=[cc_in.ap().opt()], outs=[cc_out.ap().opt()])
            sch.add("gpsimd", coll, waits=[("WL", wl_cc)], inc="CC")
            dma("gpsimd", gth[:], cc_out.ap(), "WL", waits=[("CC", sch.val("CC"))])
            WLGTH[0] = sch.val("WL")
            for tm in range(NMAIN):
                def mmc(e, tm=tm):
                    return e.matmul(P3[:, 128 + tm * E:128 + (tm + 1) * E], slo[:],
                                    mball[:, tm * E:(tm + 1) * E], start=True, stop=True)
                sch.add("tensor", mmc, inc="T")
            t_cum = sch.val("T")
            for tm in range(NMAIN):
                sch.add("vector", lambda e, tm=tm: e.scalar_tensor_tensor(
                    cnts[:], m16[:, tm * E:(tm + 1) * E], 1.0,
                    P3[:, 128 + tm * E:128 + (tm + 1) * E],
                    ALU.mult, ALU.mult),
                    waits=[("T", t_cum)] if tm == 0 else [], inc="V")
                sch.add("vector", lambda e, tm=tm: e.reduce_sum(
                    oc16[:, tm:tm + 1], cnts[:], axis=AX.X), inc="V")
                v_oc[0] = sch.val("V")

        def router_part2():
            sch.add("tensor", lambda e: e.matmul(P3[0:1, 256:256 + E], sel[:], gth[:],
                                                 start=True, stop=True),
                    waits=[("WL", WLGTH[0])], inc="T")
            t_pre = sch.val("T")
            sch.add("vector", lambda e: e.tensor_copy(prow[:], P3[0:1, 256:256 + E]),
                    waits=[("T", t_pre)], inc="V")
            v_prow = sch.val("V")
            for tm in range(NMAIN):
                sch.add("vector", lambda e, tm=tm: e.tensor_tensor(
                    cprow[:, tm * E:(tm + 1) * E], snapc[:, tm * E:(tm + 1) * E],
                    prow[:], ALU.add),
                    waits=[("V", v_prow)] if tm == 0 else [], inc="V")
            v_cpr = sch.val("V")
            sch.add("tensor", lambda e: e.matmul(P3[:, 128:256], onr32[:], cprow[:],
                                                 start=True, stop=True),
                    waits=[("V", v_cpr)], inc="T")
            t_bc = sch.val("T")
            for tm in range(NMAIN):
                sch.add("vector", lambda e, tm=tm: e.scalar_tensor_tensor(
                    cnts[:], m16[:, tm * E:(tm + 1) * E], 1.0,
                    P3[:, 128 + tm * E:128 + (tm + 1) * E], ALU.mult, ALU.mult),
                    waits=[("T", t_bc)] if tm == 0 else [], inc="V")
                sch.add("vector", lambda e, tm=tm: e.reduce_sum(
                    op16[:, tm:tm + 1], cnts[:], axis=AX.X), inc="V")
            sch.add("vector", lambda e: e.tensor_tensor(sum16[:], oc16[:], op16[:], ALU.add),
                    inc="V")
            sch.add("vector", lambda e: e.tensor_scalar(keep16[:], sum16[:], CAP, None,
                                                        ALU.is_lt), inc="V")
            sch.add("vector", lambda e: e.tensor_tensor(kg16[:], keep16[:], gate16[:],
                                                        ALU.mult), inc="V")
            VKG[0] = sch.val("V")

        R = [False, False]
        T_QDONE = [0]
        WLGTH = [0]
        v_oc = [0]
        chunks = [(bi, c) for bi in range(5) for c in range(NCH)]
        ci = [0]

        def maybe_b():
            if ci[0] < len(chunks):
                bi, c = chunks[ci[0]]
                off, ln = BLKS[bi]
                if (off + ln - 1) // 128 < next_m[0]:
                    stage_b(bi, c)
                    ci[0] += 1
                    if c == NCH - 1:
                        blk_done[0] = bi
                    return True
            return False

        tog = [0]
        while next_a[0] < NT:
            maybe_b()
            maybe_a()
        router_part1()
        R[0] = True
        while ci[0] < len(chunks):
            maybe_b()
            if not R[1] and blk_done[0] >= 2:
                router_part2()
                R[1] = True
            if tog[0] % 2 == 0:
                maybe_c1()
            elif R[1]:
                maybe_c2()
            tog[0] += 1
        if not R[1]:
            router_part2()
            R[1] = True
        while done_c1[0] < NMAIN or c2_next[0] < NMAIN:
            maybe_c1()
            maybe_c2()

        sch.add("sync", lambda e: e.nop(), waits=[("ST", n_store[0])])

        sems = {}
        for s_name in sch.sem_names():
            sems[s_name] = es.enter_context(nc.semaphore(s_name.replace(".", "_")))
        block = es.enter_context(nc.Block())
        block.sync(lambda e: sch.emit_body(nc, sems, "sync"))
        block.gpsimd(lambda e: sch.emit_body(nc, sems, "gpsimd"))
        block.scalar(lambda e: sch.emit_body(nc, sems, "scalar"))
        block.vector(lambda e: sch.emit_body(nc, sems, "vector"))
        block.tensor(lambda e: sch.emit_body(nc, sems, "tensor"))

    return nc


_NC_CACHE = None


def kernel(L0_attn_x, L0_attn_W_in, L0_attn_b_in, L0_attn_log_A, L0_attn_B_p,
           L0_attn_C_p, L0_attn_W_out, L0_attn_b_out, L0_attn_gamma, L0_attn_beta,
           L0_ffn_Wg, L0_ffn_We, L0_ffn_be):
    global _NC_CACHE
    x = np.ascontiguousarray(np.asarray(L0_attn_x, np.float32).reshape(16384, D))

    if _NC_CACHE is None:
        _NC_CACHE = build_nc()
    nc = _NC_CACHE

    common = {
        "w_in": np.ascontiguousarray(np.asarray(L0_attn_W_in, np.float32)),
        "w_out": np.ascontiguousarray(np.asarray(L0_attn_W_out, np.float32)),
        "w_e": np.ascontiguousarray(np.asarray(L0_ffn_We, np.float32)),
        "w_g": np.ascontiguousarray(np.asarray(L0_ffn_Wg, np.float32)),
        "b_in": np.asarray(L0_attn_b_in, np.float32).reshape(1, D),
        "b_in_c": np.ascontiguousarray(np.asarray(L0_attn_b_in, np.float32).reshape(E, 128).T),
        "b_out": np.asarray(L0_attn_b_out, np.float32).reshape(1, D),
        "b_e": np.asarray(L0_ffn_be, np.float32).reshape(1, D),
        "log_a": np.ascontiguousarray(np.asarray(L0_attn_log_A, np.float32).reshape(E, 128).T),
        "b_p": np.ascontiguousarray(np.asarray(L0_attn_B_p, np.float32).reshape(E, 128).T),
        "c_p": np.ascontiguousarray(np.asarray(L0_attn_C_p, np.float32).reshape(E, 128).T),
        "gam_bc": np.ascontiguousarray(np.broadcast_to(
            np.asarray(L0_attn_gamma, np.float32).reshape(1, D), (128, D))),
        "bet_bc": np.ascontiguousarray(np.broadcast_to(
            np.asarray(L0_attn_beta, np.float32).reshape(1, D), (128, D))),
        "identb": np.eye(128, dtype=np.float32).astype(BF),
        "slob": np.triu(np.ones((128, 128), np.float32), 1).astype(BF),
        "onescol_b": np.ones((128, 1), np.float32).astype(BF),
        "onesrow_b": np.ones((1, 128), np.float32).astype(BF),
        "onesrow32": np.ones((1, 128), np.float32),
    }
    in_maps = []
    for k in range(NC):
        start = k * SLICE
        xsl = np.zeros((TOK, D), np.float32)
        lo = max(0, start - WARM)
        xsl[WARM - (start - lo):] = x[lo:start + SLICE]
        brow = np.ones((1, TOK), np.float32)
        if k == 0:
            brow[0, :WARM] = 0.0
        selv = np.zeros((8, 1), np.float32)
        selv[:k, 0] = 1.0
        im = dict(common)
        im["xs"] = xsl
        im["browb"] = brow.astype(BF)
        im["sel8"] = selv
        in_maps.append(im)

    res = run_bass_kernel_spmd(nc, in_maps, core_ids=list(range(NC)))
    out = np.concatenate([res.results[k]["y_out"] for k in range(NC)], axis=0)
    return np.ascontiguousarray(out).reshape(1, 16384, D)


# version 27 (final)
# speedup vs baseline: 1.0630x; 1.0093x over previous
import numpy as np
import ml_dtypes
import concourse.bass as bass
import concourse.mybir as mybir
from concourse.bass_utils import run_bass_kernel_spmd
from contextlib import ExitStack

F32 = mybir.dt.float32
BF16 = mybir.dt.bfloat16
AF = mybir.ActivationFunctionType
ALU = mybir.AluOpType
AX = mybir.AxisListType
BF = ml_dtypes.bfloat16

D = 1024
NC = 8
SLICE = 2048
WARM = 256
TOK = SLICE + WARM
NT = TOK // 128
NWARM = WARM // 128
NMAIN = SLICE // 128
NCH = D // 128
CAP = 2048.0
EPS = 1e-5
E = 8
BLKS = [(0, 512), (512, 512), (1024, 512), (1536, 512), (2048, 256)]
YW = 3


def _incby(sem):
    return 16 if sem[:2] in ("LD", "WL", "XF", "ST", "XP") else 1


class Sched:

    def __init__(self):
        self.ops = []
        self.cnt = {}
        self.wait_claims = {}

    def val(self, sem):
        return self.cnt.get(sem, 0)

    def add(self, eng, fn, waits=(), inc=None):
        new_cnt = None
        if inc is not None:
            new_cnt = self.cnt.get(inc, 0) + _incby(inc)
        rep = (inc, new_cnt) if inc is not None else None
        fixed = []
        for base, v in waits:
            if v <= 0:
                continue
            guard = 0
            while not (_incby(base) == 16 or base == "CC"):
                claim = self.wait_claims.get((base, v))
                if claim is None:
                    self.wait_claims[(base, v)] = (eng, rep)
                    break
                if claim[0] == eng:
                    break
                if claim[1] is None:
                    break
                base, v = claim[1]
                guard += 1
                assert guard < 64, "wait chaining loop"
            fixed.append((base, v))
        if inc is not None:
            self.cnt[inc] = new_cnt
        self.ops.append((eng, fn, tuple(fixed), inc))

    def variant_plan(self):
        plan = {}
        for eng, _, waits, _ in self.ops:
            for sem_name, v in waits:
                if _incby(sem_name) == 16 or sem_name == "CC":
                    key = (sem_name, None)
                else:
                    key = (sem_name, eng)
                plan.setdefault(key, set()).add(v)
        return plan

    def sem_names(self):
        names = {"V.self", "S.self"}
        for (base, eng), _ in self.variant_plan().items():
            names.add(base if eng is None else f"{base}.{eng}")
        return sorted(names)

    def emit_body(self, nc, sems, eng_name):
        eng = {"sync": nc.sync, "gpsimd": nc.gpsimd, "scalar": nc.scalar,
               "vector": nc.vector, "tensor": nc.tensor}[eng_name]
        plan = self.variant_plan()
        variants = {}
        for (base, ceng), vals in plan.items():
            if ceng is not None:
                variants.setdefault(base, []).append((f"{base}.{ceng}", vals))
        self_chain = eng_name in ("vector", "scalar")
        selfsem = {"vector": "V.self", "scalar": "S.self"}.get(eng_name)
        selfval = 0
        cnt = 0
        due = []
        pend = {}
        prev_attach = None
        seen = {}
        for name, fn, waits, inc in self.ops:
            if name != eng_name:
                continue
            wlist = []
            for sem_name, v in waits:
                if not (_incby(sem_name) == 16 or sem_name == "CC"):
                    sem_name = f"{sem_name}.{eng_name}"
                wlist.append((sem_name, v))
            if self_chain and prev_attach is not None:
                wlist.append(prev_attach)
            for sem_name, v in wlist:
                if seen.get(sem_name, 0) < v:
                    eng.wait_ge(sems[sem_name], v)
                    seen[sem_name] = v
            ins = fn(eng)
            if inc is None:
                continue
            step = _incby(inc)
            if step == 16 or inc == "CC":
                ins.then_inc(sems[inc], step)
                continue
            cnt += step
            for vs_name, vals in variants.get(inc, ()):
                pend[vs_name] = pend.get(vs_name, 0) + step
                if cnt in vals and vs_name not in due:
                    due.append(vs_name)
            if due:
                vs = due.pop(0)
                ins.then_inc(sems[vs], pend[vs])
                prev_attach = (vs, cnt)
                pend[vs] = 0
            elif self_chain:
                selfval += 1
                ins.then_inc(sems[selfsem], 1)
                prev_attach = (selfsem, selfval)
        assert not due, f"unflushed sem variants on {eng_name}: {due}"


def build_nc(upto=99):
    nc = bass.Bass("TRN2", target_bir_lowering=False, detect_race_conditions=False)

    xs = nc.dram_tensor("xs", [TOK, D], F32, kind="ExternalInput").ap()
    w_in = nc.dram_tensor("w_in", [D, D], F32, kind="ExternalInput").ap()
    w_out = nc.dram_tensor("w_out", [D, D], F32, kind="ExternalInput").ap()
    w_e = nc.dram_tensor("w_e", [D, D], F32, kind="ExternalInput").ap()
    w_g = nc.dram_tensor("w_g", [D, E], F32, kind="ExternalInput").ap()
    b_in = nc.dram_tensor("b_in", [1, D], F32, kind="ExternalInput").ap()
    b_in_c = nc.dram_tensor("b_in_c", [128, E], F32, kind="ExternalInput").ap()
    b_out = nc.dram_tensor("b_out", [1, D], F32, kind="ExternalInput").ap()
    b_e = nc.dram_tensor("b_e", [1, D], F32, kind="ExternalInput").ap()
    log_a = nc.dram_tensor("log_a", [128, E], F32, kind="ExternalInput").ap()
    b_p = nc.dram_tensor("b_p", [128, E], F32, kind="ExternalInput").ap()
    c_p = nc.dram_tensor("c_p", [128, E], F32, kind="ExternalInput").ap()
    gam_bc = nc.dram_tensor("gam_bc", [128, D], F32, kind="ExternalInput").ap()
    bet_bc = nc.dram_tensor("bet_bc", [128, D], F32, kind="ExternalInput").ap()
    identb = nc.dram_tensor("identb", [128, 128], BF16, kind="ExternalInput").ap()
    slob = nc.dram_tensor("slob", [128, 128], BF16, kind="ExternalInput").ap()
    onescol_b = nc.dram_tensor("onescol_b", [128, 1], BF16, kind="ExternalInput").ap()
    onesrow_b = nc.dram_tensor("onesrow_b", [1, 128], BF16, kind="ExternalInput").ap()
    onesrow32 = nc.dram_tensor("onesrow32", [1, 128], F32, kind="ExternalInput").ap()
    browb = nc.dram_tensor("browb", [1, TOK], BF16, kind="ExternalInput").ap()
    sel8 = nc.dram_tensor("sel8", [8, 1], F32, kind="ExternalInput").ap()
    y_out = nc.dram_tensor("y_out", [SLICE, D], F32, kind="ExternalOutput").ap()
    cc_in = nc.dram_tensor("cc_in", [1, E], F32)
    cc_out = nc.dram_tensor("cc_out", [NC, E], F32)

    es = ExitStack()
    with es:
        def sb(name, shape, dt):
            return nc.alloc_sbuf_tensor(name, shape, dt)

        xnfm = sb("xnfm", [128, NCH * TOK], BF16)
        winT = sb("winT", [128, NCH * D], BF16)
        woutT = sb("woutT", [128, NCH * D], BF16)
        yw = sb("yw", [128, YW * NCH * 512], BF16)
        gbc = sb("gbc", [128, D], F32)
        bbc = sb("bbc", [128, D], F32)
        wg32 = sb("wg32", [128, NCH * E], F32)
        wghl = sb("wghl", [128, NCH * 2 * E], BF16)
        wghf = sb("wghf", [128, NCH * E], F32)
        wgl32 = sb("wgl32", [128, NCH * E], F32)
        qtmp = sb("qtmp", [128, 3 * E], F32)
        qall = sb("qall", [128, 3 * E * NMAIN], F32)
        mball = sb("mball", [128, E * NMAIN], BF16)
        idt = sb("idt", [128, 128], BF16)
        slo = sb("slo", [128, 128], BF16)
        onc = sb("onc", [128, 1], BF16)
        onrb = sb("onrb", [1, 128], BF16)
        onr32 = sb("onr32", [1, 128], F32)
        brw = sb("brw", [1, TOK], BF16)
        sel = sb("sel", [8, 1], F32)
        binb = sb("binb", [1, D], BF16)
        boutb = sb("boutb", [1, D], BF16)
        beb = sb("beb", [1, D], BF16)
        la = sb("la", [128, E], F32)
        bpc = sb("bpc", [128, E], F32)
        cpc = sb("cpc", [128, E], F32)
        spc = sb("spc", [128, E], F32)
        dcol = sb("dcol", [128, E], F32)
        stcol = sb("stcol", [128, E], F32)
        binc = sb("binc", [128, E], F32)
        binB = sb("binB", [128, E], F32)
        m16 = sb("m16", [128, E * NMAIN], F32)
        q16 = sb("q16", [128, E * NMAIN], F32)
        ex16 = sb("ex16", [128, E * NMAIN], F32)
        nmx16 = sb("nmx16", [128, NMAIN], F32)
        oc16 = sb("oc16", [128, NMAIN], F32)
        op16 = sb("op16", [128, NMAIN], F32)
        gate16 = sb("gate16", [128, NMAIN], F32)
        es16 = sb("es16", [128, NMAIN], F32)
        kg16 = sb("kg16", [128, NMAIN], F32)
        sum16 = sb("sum16", [128, NMAIN], F32)
        keep16 = sb("keep16", [128, NMAIN], F32)
        mu16 = sb("mu16", [128, NMAIN], F32)
        var16 = sb("var16", [128, NMAIN], F32)
        sd16 = sb("sd16", [128, NMAIN], F32)
        inv16 = sb("inv16", [128, NMAIN], F32)
        nmi16 = sb("nmi16", [128, NMAIN], F32)
        carry = sb("carry", [1, E], F32)
        snapc = sb("snapc", [1, E * NMAIN], F32)
        prow = sb("prow", [1, E], F32)
        cprow = sb("cprow", [1, E * NMAIN], F32)
        gth = sb("gth", [8, E], F32)
        mbuf = sb("mbuf", [128, E], BF16)
        cnts = sb("cnts", [128, E], F32)
        sq = sb("sq", [128, D], F32)
        stat = sb("stat", [128, 24], F32)

        AREN = 83968
        base = nc.bump_sbuf(AREN)
        assert base is not None, "SBUF arena allocation failed"
        off0 = base[0]

        def at(name, off, shape, dt):
            return nc.alloc_sbuf_tensor_at(name, shape, dt, offset=off0 + off)

        xb = [at("xb0", 0, [128, D], F32), at("xb1", 4096, [128, D], F32)]
        xhi = [at("xhi0", 8192, [128, D], BF16), at("xhi1", 10240, [128, D], BF16)]
        xlo = [at("xlo0", 12288, [128, D], BF16), at("xlo1", 14336, [128, D], BF16)]
        xhi32 = at("xh32", 16384, [128, D], F32)
        xlo32 = at("xl32", 20480, [128, D], F32)
        wesb = at("wesb", 8192, [128, NCH * D], BF16)
        xfl = [at("xfl0", 24576, [128, NCH * 128], BF16)]
        d1 = [at("d1a", 26624, [128, 512], F32), at("d1b", 28672, [128, 512], F32)]
        ssb = [at("ssb0", 30720, [128, 512], F32), at("ssb1", 81920, [128, 512], F32)]
        NOUT = 16
        outb = at("outb", 32768, [128, NOUT * D], BF16)
        sbs = [at("sbs0", 65536, [128, D], F32), at("sbs1", 69632, [128, D], F32)]
        gbuf = [at("gb0", 73728, [128, D], F32), at("gb1", 77824, [128, D], F32)]
        xre = xb
        wst = at("wst", 32768, [128, NCH * D], BF16)

        P01 = nc.alloc_psum_tensor("P01", [128, 1024], F32)
        P2 = nc.alloc_psum_tensor("P2", [128, 512], F32)
        P3 = nc.alloc_psum_tensor("P3", [128, 512], F32)
        P4 = nc.alloc_psum_tensor("P4", [128, 512], F32)
        P5 = nc.alloc_psum_tensor("P5", [128, 512], F32)
        P67 = nc.alloc_psum_tensor("P67", [128, 1024], F32)

        sch = Sched()

        def dma(eng_name, dst, src, sem, waits=()):
            sch.add(eng_name, lambda e, d=dst, s=src: e.dma_start(out=d, in_=s),
                    waits=waits, inc=sem)

        dma("sync", gbc[:], gam_bc, "LDC")
        dma("sync", bbc[:], bet_bc, "LDC")
        dma("sync", idt[:], identb, "LDC")
        dma("sync", slo[:], slob, "LDC")
        dma("sync", onc[:], onescol_b, "LDC")
        dma("sync", onrb[:], onesrow_b, "LDC")
        dma("sync", onr32[:], onesrow32, "LDC")
        dma("sync", brw[:], browb, "LDC")
        dma("sync", sel[:], sel8, "LDC")
        dma("sync", la[:], log_a, "LDC")
        dma("sync", bpc[:], b_p, "LDC")
        dma("sync", cpc[:], c_p, "LDC")
        dma("sync", binc[:], b_in_c, "LDC")
        dma("sync", wg32[:].rearrange("p (c j) -> p c j", j=E),
            w_g.rearrange("(c p) j -> p c j", p=128), "LDC")
        LD_CONST = sch.val("LDC")
        dma("gpsimd", binb[:], b_in, "WL")
        dma("gpsimd", boutb[:], b_out, "WL")
        dma("gpsimd", beb[:], b_e, "WL")
        dma("gpsimd", wst[:].rearrange("p (c d) -> p c d", d=D),
            w_in.rearrange("(c p) d -> p c d", p=128), "WL")
        WL_WIN = sch.val("WL")

        wconst = [("LDC", LD_CONST)]
        sch.add("scalar", lambda e: e.activation(spc[:], la[:], AF.Exp), waits=wconst, inc="S")
        s_e1 = sch.val("S")
        sch.add("vector", lambda e: e.tensor_scalar(spc[:], spc[:], 1.0, None, ALU.add),
                waits=[("S", s_e1)], inc="V")
        v_e1 = sch.val("V")
        sch.add("scalar", lambda e: e.activation(spc[:], spc[:], AF.Ln),
                waits=[("V", v_e1)], inc="S")
        s_ln = sch.val("S")
        sch.add("scalar", lambda e: e.activation(dcol[:], spc[:], AF.Exp, scale=-1.0),
                waits=[("S", s_ln)], inc="S")
        S_DECAY = sch.val("S")
        wghl3 = wghl[:].rearrange("p (c j) -> p c j", j=2 * E)
        wg32_3 = wg32[:].rearrange("p (c j) -> p c j", j=E)
        wghf3 = wghf[:].rearrange("p (c j) -> p c j", j=E)
        wgl32_3 = wgl32[:].rearrange("p (c j) -> p c j", j=E)
        sch.add("vector", lambda e: e.tensor_copy(wghl3[:, :, 0:E], wg32_3), waits=wconst, inc="V")
        v1 = sch.val("V")
        sch.add("vector", lambda e: e.tensor_copy(wghf3, wghl3[:, :, 0:E]),
                waits=[("V", v1)], inc="V")
        v1b = sch.val("V")
        sch.add("vector", lambda e: e.tensor_tensor(wgl32[:], wg32[:], wghf[:], ALU.subtract),
                waits=[("V", v1b)], inc="V")
        sch.add("vector", lambda e: e.tensor_copy(wghl3[:, :, E:2 * E], wgl32_3), inc="V")
        V_WG = sch.val("V")
        sch.add("vector", lambda e: e.memset(stcol[:], 0.0), inc="V")
        sch.add("vector", lambda e: e.memset(carry[:], 0.0), inc="V")
        sch.add("vector", lambda e: e.tensor_mul(binB[:], binc[:], bpc[:]), waits=wconst, inc="V")
        V_ST0 = sch.val("V")

        def w_transpose(dst_sb, wait_first, prev_ev=None):
            ev = {}
            for dc in range(NCH):
                slot = dc % 2
                half_ev = 0
                for er in range(NCH):
                    w = []
                    if er == 0:
                        w = list(wait_first)
                        if dc >= 2:
                            w.append(("S", ev[dc - 2]))
                        elif prev_ev is not None:
                            w.append(("S", prev_ev[6 + slot]))
                    elif er == 4:
                        w = [("S", half_ev)]
                    def mm(e, er=er, dc=dc, slot=slot):
                        return e.matmul(P67[:, slot * 512 + (er % 4) * 128:
                                            slot * 512 + ((er % 4) + 1) * 128],
                                        wst[:, er * D + dc * 128: er * D + (dc + 1) * 128],
                                        idt[:], start=True, stop=True)
                    sch.add("tensor", mm, waits=w, inc="T")
                    if er in (3, 7):
                        t_done = sch.val("T")
                        half = 0 if er == 3 else 1
                        def evc(e, dc=dc, slot=slot, dst_sb=dst_sb, half=half):
                            return e.activation(
                                dst_sb[:, dc * D + half * 512: dc * D + half * 512 + 512],
                                P67[:, slot * 512:(slot + 1) * 512], AF.Copy)
                        sch.add("scalar", evc, waits=[("T", t_done)], inc="S")
                        half_ev = sch.val("S")
                ev[dc] = sch.val("S")
            return ev

        ev_win = w_transpose(winT, [("WL", WL_WIN), ("LDC", LD_CONST)])
        dma("gpsimd", wst[:].rearrange("p (c d) -> p c d", d=D),
            w_out.rearrange("(c p) d -> p c d", p=128), "WL",
            waits=[("S", ev_win[7]), ("WL", WL_WIN)])
        WL_WOUT = sch.val("WL")
        ev_wout = w_transpose(woutT, [("WL", WL_WOUT)], prev_ev=ev_win)
        S_WIN = max(ev_win[6], ev_win[7])
        S_WPREP = max(ev_wout[6], ev_wout[7])

        COL = {k: i for i, k in enumerate(
            ["s1", "mu", "mu2", "var", "sd", "inv", "nmui", "s2"])}

        def st(col, sl):
            return stat[:, sl * 8 + COL[col]:sl * 8 + COL[col] + 1]

        x_free = [("V", 0)] * 2
        xhi_free = [("T", 0)] * 2
        xlo_free = [("T", 0)] * 2
        xfl_free = [("T", 0)]
        evA_hi = {}
        q_free = [("V", 0)]
        ps_hi_free = [("S", 0)]
        ps_lo_free = [("S", 0)]
        t_tot_prev = [0]

        stat_wfree = [("S", 0), ("S", 0)]
        front_nmui = {}

        def a_front(t):
            sl = t % 2
            ldx = f"LDX{sl}"
            sch.add("sync", lambda e, t=t, sl=sl: e.dma_start(
                out=xb[sl][:], in_=xs[t * 128:(t + 1) * 128, :]),
                waits=[x_free[sl]], inc=ldx)
            wx = [(ldx, sch.val(ldx))]
            sch.add("vector", lambda e, sl=sl: e.reduce_sum(st("s1", sl), xb[sl][:], axis=AX.X),
                    waits=wx, inc="V")
            v_s1 = sch.val("V")
            sch.add("scalar", lambda e, sl=sl: e.activation(sq[:], xb[sl][:], AF.Square,
                                                            accum_out=st("s2", sl)),
                    waits=wx + [stat_wfree[sl]], inc="S")
            s_s2 = sch.val("S")
            sch.add("vector", lambda e, sl=sl: e.tensor_scalar(st("mu", sl), st("s1", sl),
                                                               1.0 / D, None, ALU.mult),
                    waits=[("V", v_s1)], inc="V")
            sch.add("vector", lambda e, sl=sl: e.tensor_mul(st("mu2", sl), st("mu", sl),
                                                            st("mu", sl)), inc="V")
            sch.add("vector", lambda e, sl=sl: e.tensor_scalar(st("var", sl), st("s2", sl),
                                                               1.0 / D, EPS, ALU.mult, ALU.add),
                    waits=[("S", s_s2)], inc="V")
            sch.add("vector", lambda e, sl=sl: e.tensor_sub(st("var", sl), st("var", sl),
                                                            st("mu2", sl)), inc="V")
            v_var = sch.val("V")
            sch.add("scalar", lambda e, sl=sl: e.activation(st("sd", sl), st("var", sl), AF.Ln),
                    waits=[("V", v_var)], inc="S")
            sch.add("scalar", lambda e, sl=sl: e.activation(st("inv", sl), st("sd", sl), AF.Exp,
                                                            scale=-0.5), inc="S")
            s_inv = sch.val("S")
            sch.add("vector", lambda e, sl=sl: e.tensor_scalar(st("nmui", sl), st("mu", sl),
                                                               -1.0, st("inv", sl),
                                                               ALU.mult, ALU.mult),
                    waits=[("S", s_inv)], inc="V")
            front_nmui[t] = sch.val("V")

        lo_state = {}

        def a_mid(t):
            sl = t % 2
            main = t >= NWARM
            tm = t - NWARM
            v_nmui = front_nmui[t]
            sch.add("scalar", lambda e, sl=sl: e.activation(xhi[sl][:], xb[sl][:], AF.Identity,
                                                            bias=st("nmui", sl),
                                                            scale=st("inv", sl)),
                    waits=[("V", v_nmui), xhi_free[sl]], inc="S")
            s_hi0 = sch.val("S")
            if main:
                sch.add("vector", lambda e, sl=sl: e.tensor_copy(xhi32[:], xhi[sl][:]),
                        waits=[("S", s_hi0)], inc="V")
                sch.add("vector", lambda e, sl=sl: e.scalar_tensor_tensor(
                    xlo32[:], xb[sl][:], st("inv", sl), xhi32[:], ALU.mult, ALU.subtract),
                    inc="V")
                sch.add("vector", lambda e, sl=sl: e.tensor_scalar(
                    xlo[sl][:], xlo32[:], st("nmui", sl), None, ALU.add),
                    waits=[xlo_free[sl]], inc="V")
                v_lo = sch.val("V")
                x_free[sl] = ("V", v_lo)
                stat_wfree[sl] = ("V", v_lo)
            else:
                v_lo = 0
                x_free[sl] = ("S", s_hi0)
                stat_wfree[sl] = ("S", s_hi0)
            wtr = [("S", s_hi0)] + [ps_hi_free[0]]
            for c in range(NCH):
                def mmh(e, c=c, sl=sl):
                    return e.matmul(P01[:, c * 128:(c + 1) * 128],
                                    xhi[sl][:, c * 128:(c + 1) * 128], idt[:],
                                    start=True, stop=True)
                sch.add("tensor", mmh, waits=wtr if c == 0 else [], inc="T")
            t_hi = sch.val("T")
            xhi_free[sl] = ("T", t_hi)
            xnfm3 = xnfm[:].rearrange("p (c m) -> p c m", m=TOK)
            def evc(e, t=t, xnfm3=xnfm3):
                return e.activation(xnfm3[:, :, t * 128:(t + 1) * 128],
                                    P01[:].rearrange("p (c m) -> p c m", m=128), AF.Copy)
            sch.add("scalar", evc, waits=[("T", t_hi)], inc="S")
            evA_hi[t] = ("S", sch.val("S"))
            ps_hi_free[0] = evA_hi[t]
            lo_state[t] = (sl, v_lo)

        def a_lo(t):
            sl, v_lo = lo_state.pop(t)
            main = t >= NWARM
            tm = t - NWARM
            if main:
                for half in range(2):
                    wlo = [("V", v_lo), ps_lo_free[0]]
                    for c in range(4):
                        cc = half * 4 + c
                        def mml(e, cc=cc, sl=sl):
                            return e.matmul(P2[:, (cc % 4) * 128:((cc % 4) + 1) * 128],
                                            xlo[sl][:, cc * 128:(cc + 1) * 128], idt[:],
                                            start=True, stop=True)
                        sch.add("tensor", mml, waits=wlo if c == 0 else [], inc="T")
                    t_lo = sch.val("T")
                    w2 = [("T", t_lo)]
                    if half == 0:
                        w2.append(xfl_free[0])
                    sch.add("scalar", lambda e, half=half: e.activation(
                        xfl[0][:, half * 512:(half + 1) * 512], P2[:], AF.Copy),
                        waits=w2, inc="S")
                    ps_lo_free[0] = ("S", sch.val("S"))
                xlo_free[sl] = ("T", t_lo)
                wq = [evA_hi[t], ps_lo_free[0], ("V", V_WG), q_free[0]]
                for c in range(NCH):
                    first = c == 0
                    def mmq1(e, c=c, t=t, first=first, last=(c == NCH - 1)):
                        return e.matmul(P3[:, 0:2 * E],
                                        xnfm[:, c * TOK + t * 128: c * TOK + (t + 1) * 128],
                                        wghl[:, c * 2 * E:(c + 1) * 2 * E],
                                        start=first, stop=last)
                    sch.add("tensor", mmq1, waits=wq if first else [], inc="T")
                for c in range(NCH):
                    def mmq3(e, c=c, first=(c == 0), last=(c == NCH - 1)):
                        return e.matmul(P3[:, 2 * E:3 * E], xfl[0][:, c * 128:(c + 1) * 128],
                                        wghl[:, c * 2 * E: c * 2 * E + E],
                                        start=first, stop=last)
                    sch.add("tensor", mmq3, inc="T")
                t_q = sch.val("T")
                xfl_free[0] = ("T", t_q)
                sch.add("vector", lambda e, tm=tm: e.tensor_copy(
                    qall[:, tm * 3 * E:(tm + 1) * 3 * E], P3[:, 0:3 * E]),
                    waits=[("T", t_q)], inc="V")
                q_free[0] = ("V", sch.val("V"))
            return

        d1_free = [("V", 0), ("V", 0)]
        ssb_free = [("S", 0), ("S", 0)]
        ps_h_free = [("S", 0), ("S", 0)]
        bu = [0]
        yev_blk = {}
        yw_free = {}
        last_scan = [0]

        def stage_b(bi, c):
            off, ln = BLKS[bi]
            u = bu[0]
            bu[0] += 1
            hsl = u % 2
            dsl = u % 2
            PH = P4 if hsl == 0 else P5
            tiles_needed = (off + ln - 1) // 128
            wh = [evA_hi[tiles_needed], ps_h_free[hsl]]
            if bi == 0 and c == 0:
                wh.append(("S", S_WIN))
            for dc in range(NCH):
                first = dc == 0
                last = (dc == NCH - 1) and bi != 0
                def mmh2(e, c=c, dc=dc, off=off, ln=ln, PH=PH, first=first, last=last):
                    return e.matmul(PH[:, 0:ln],
                                    winT[:, dc * D + c * 128: dc * D + (c + 1) * 128],
                                    xnfm[:, dc * TOK + off: dc * TOK + off + ln],
                                    start=first, stop=last)
                sch.add("tensor", mmh2, waits=wh if first else [], inc="T")
            if bi == 0:
                def mmb(e, c=c, off=off, ln=ln, PH=PH):
                    return e.matmul(PH[:, 0:ln], binb[0:1, c * 128:(c + 1) * 128],
                                    brw[0:1, off:off + ln], start=False, stop=True)
                sch.add("tensor", mmb, inc="T")
            t_h = sch.val("T")
            def evd(e, c=c, ln=ln, PH=PH, dsl=dsl, bi=bi):
                if bi == 0:
                    return e.activation(d1[dsl][:, 0:ln], PH[:, 0:ln],
                                        AF.Identity, scale=bpc[:, c:c + 1])
                return e.activation(d1[dsl][:, 0:ln], PH[:, 0:ln],
                                    AF.Identity, scale=bpc[:, c:c + 1],
                                    bias=binB[:, c:c + 1])
            sch.add("scalar", evd, waits=[("T", t_h), d1_free[dsl]], inc="S")
            ps_h_free[hsl] = ("S", sch.val("S"))
            s_d1 = sch.val("S")
            ssl = u % 2
            def scn(e, c=c, ln=ln, dsl=dsl, bi=bi, ssl=ssl):
                init = 0.0 if bi == 0 else stcol[:, c:c + 1]
                return e.tensor_tensor_scan(ssb[ssl][:, 0:ln],
                                            dcol[:, c:c + 1].broadcast_to([128, ln]),
                                            d1[dsl][:, 0:ln], init, ALU.mult, ALU.add)
            sch.add("vector", scn, waits=[("S", s_d1), ssb_free[ssl], ("V", V_ST0)], inc="V")
            v_scan = sch.val("V")
            last_scan[0] = v_scan
            d1_free[dsl] = ("V", v_scan)
            sch.add("vector", lambda e, c=c, ln=ln, ssl=ssl: e.tensor_copy(
                stcol[:, c:c + 1], ssb[ssl][:, ln - 1:ln]), inc="V")
            v_st = sch.val("V")
            mstart = max(off, WARM)
            if off + ln > WARM:
                wv = [("V", v_st)]
                if bi >= YW and c == 0:
                    wv.append(yw_free.get(bi % YW, ("T", 0)))
                def evy(e, c=c, off=off, ln=ln, bi=bi, mstart=mstart, ssl=ssl):
                    return e.activation(
                        yw[:, (bi % YW) * NCH * 512 + c * 512 + (mstart - off):
                           (bi % YW) * NCH * 512 + c * 512 + ln],
                        ssb[ssl][:, mstart - off:ln],
                        AF.Identity, scale=cpc[:, c:c + 1])
                sch.add("scalar", evy, waits=wv, inc="S")
            ssb_free[ssl] = ("S", sch.val("S"))
            if c == NCH - 1:
                yev_blk[bi] = sch.val("S")

        ps_o_free = [("S", 0)]
        c1_done = {}
        prev_c1v = [0]
        outb_free = {}

        def stage_c1(tm):
            qi = tm
            bi = (WARM + tm * 128) // 512
            ywoff = (bi % YW) * NCH * 512
            inoff = WARM + tm * 128 - bi * 512
            w0 = [("S", yev_blk[bi]), ps_o_free[0]]
            if tm == 0:
                w0.append(("S", S_WPREP))
            for half in range(2):
                for ec in range(NCH):
                    first = ec == 0
                    def mmo(e, tm=tm, ec=ec, half=half, ywoff=ywoff, inoff=inoff, first=first):
                        return e.matmul(P67[:, half * 512:(half + 1) * 512],
                                        yw[:, ywoff + ec * 512 + inoff:
                                           ywoff + ec * 512 + inoff + 128],
                                        woutT[:, ec * D + half * 512: ec * D + half * 512 + 512],
                                        start=first, stop=False)
                    sch.add("tensor", mmo, waits=w0 if (first and half == 0) else [], inc="T")
                def mmob(e, half=half):
                    return e.matmul(P67[:, half * 512:(half + 1) * 512], onrb[:],
                                    boutb[0:1, half * 512: half * 512 + 512],
                                    start=False, stop=True)
                sch.add("tensor", mmob, inc="T")
            t_out = sch.val("T")
            yw_free[bi % YW] = ("T", t_out)
            wev = [("T", t_out)]
            sch.add("scalar", lambda e, qi=qi: e.activation(
                outb[:, qi * D:(qi + 1) * D], P67[:], AF.Copy), waits=wev, inc="S")
            s_ev = sch.val("S")
            ps_o_free[0] = ("S", s_ev)
            sch.add("scalar", lambda e, qi=qi: e.activation(
                sq[:], outb[:, qi * D:(qi + 1) * D], AF.Square, accum_out=stat[:, 17:18]),
                waits=[("V", prev_c1v[0])], inc="S")
            s_sq = sch.val("S")
            sch.add("vector", lambda e, qi=qi: e.reduce_sum(
                stat[:, 16:17], outb[:, qi * D:(qi + 1) * D], axis=AX.X),
                waits=[("S", s_ev)], inc="V")
            v_red = sch.val("V")
            sch.add("vector", lambda e, tm=tm: e.tensor_scalar(
                mu16[:, tm:tm + 1], stat[:, 16:17], 1.0 / D, None, ALU.mult),
                waits=[("V", v_red), ("S", s_sq)], inc="V")
            sch.add("vector", lambda e: e.tensor_scalar(stat[:, 18:19], stat[:, 17:18],
                                                        1.0 / D, EPS, ALU.mult, ALU.add), inc="V")
            sch.add("vector", lambda e, tm=tm: e.tensor_mul(stat[:, 19:20], mu16[:, tm:tm + 1],
                                                            mu16[:, tm:tm + 1]), inc="V")
            sch.add("vector", lambda e, tm=tm: e.tensor_sub(var16[:, tm:tm + 1], stat[:, 18:19],
                                                            stat[:, 19:20]), inc="V")
            prev_c1v[0] = sch.val("V")
            c1_done[tm] = prev_c1v[0]

        inv_done = {}

        def stage_cmid(g):
            v_allvar = c1_done[2 * g + 1]
            gs = slice(2 * g, 2 * g + 2)
            sch.add("scalar", lambda e, gs=gs: e.activation(sd16[:, gs], var16[:, gs], AF.Ln),
                    waits=[("V", v_allvar)], inc="S")
            sch.add("scalar", lambda e, gs=gs: e.activation(inv16[:, gs], sd16[:, gs], AF.Exp,
                                                            scale=-0.5), inc="S")
            s_i4 = sch.val("S")
            sch.add("vector", lambda e, gs=gs: e.tensor_scalar(nmi16[:, gs], mu16[:, gs],
                                                               -1.0, None, ALU.mult),
                    waits=[("S", s_i4)], inc="V")
            sch.add("vector", lambda e, gs=gs: e.tensor_mul(nmi16[:, gs], nmi16[:, gs],
                                                            inv16[:, gs]), inc="V")
            inv_done[g] = sch.val("V")

        xre_free = [None, None]
        s_free = [("S", 0)] * 2
        g_free = [("V", 0)] * 2
        ps_z_free = [("S", 0)]
        rb_free = [("ST", 0), ("ST", 0)]
        n_store = [0]
        VKG = [0]
        SAEND = [0]
        WLWE = [0]

        def stage_c2(tm):
            sl = tm % 2
            qi = tm
            g = tm // 2
            xfs = f"XF{sl}"
            wx = [xre_free[sl]] if xre_free[sl] else [("V", VKG[0])]
            sch.add("sync", lambda e, tm=tm, sl=sl: e.dma_start(
                out=xre[sl][:], in_=xs[WARM + tm * 128: WARM + (tm + 1) * 128, :]),
                waits=wx, inc=xfs)
            xf_t = sch.val(xfs)
            sch.add("scalar", lambda e, qi=qi, sl=sl, tm=tm: e.activation(
                sbs[sl][:], outb[:, qi * D:(qi + 1) * D], AF.Identity,
                bias=nmi16[:, tm:tm + 1], scale=inv16[:, tm:tm + 1]),
                waits=[("V", inv_done[g]), ("V", VKG[0]), s_free[sl]], inc="S")
            s_app = sch.val("S")
            sch.add("vector", lambda e, sl=sl: e.tensor_tensor(sbs[sl][:], sbs[sl][:], gbc[:],
                                                               ALU.mult),
                    waits=[("S", s_app)], inc="V")
            sch.add("vector", lambda e, sl=sl: e.tensor_tensor(sbs[sl][:], sbs[sl][:], bbc[:],
                                                               ALU.add), inc="V")
            v_s = sch.val("V")
            for half in range(2):
                for dc in range(NCH):
                    first = dc == 0
                    w = []
                    if first and half == 0:
                        w = [ps_z_free[0], ("WL", WLWE[0])]
                        if tm == 0:
                            w.append(evA_hi[NT - 1])
                    def mmz(e, tm=tm, dc=dc, half=half, first=first):
                        return e.matmul(P01[:, half * 512:(half + 1) * 512],
                                        xnfm[:, dc * TOK + WARM + tm * 128:
                                             dc * TOK + WARM + (tm + 1) * 128],
                                        wesb[:, dc * D + half * 512: dc * D + half * 512 + 512],
                                        start=first, stop=False)
                    sch.add("tensor", mmz, waits=w, inc="T")
                def mmzb(e, half=half):
                    return e.matmul(P01[:, half * 512:(half + 1) * 512], onrb[:],
                                    beb[0:1, half * 512: half * 512 + 512],
                                    start=False, stop=True)
                sch.add("tensor", mmzb, inc="T")
            t_z = sch.val("T")
            sch.add("scalar", lambda e, tm=tm, sl=sl: e.activation(
                gbuf[sl][:], P01[:], AF.Silu, scale=kg16[:, tm:tm + 1]),
                waits=[("T", t_z), g_free[sl]], inc="S")
            s_g = sch.val("S")
            ps_z_free[0] = ("S", s_g)
            sch.add("vector", lambda e, sl=sl: e.tensor_tensor(gbuf[sl][:], sbs[sl][:],
                                                               gbuf[sl][:], ALU.mult),
                    waits=[("S", s_g), ("V", v_s)], inc="V")
            sch.add("vector", lambda e, sl=sl: e.tensor_tensor(
                sbs[sl][:], gbuf[sl][:], xre[sl][:], ALU.add),
                waits=[(xfs, xf_t), rb_free[sl]], inc="V")
            v_res = sch.val("V")
            xre_free[sl] = ("V", v_res)
            g_free[sl] = ("V", v_res)
            sch.add("sync", lambda e, tm=tm, sl=sl: e.dma_start(
                out=y_out[tm * 128:(tm + 1) * 128, :], in_=sbs[sl][:]),
                waits=[("V", v_res)], inc="ST")
            n_store[0] += 16
            rb_free[sl] = ("ST", sch.val("ST"))
            s_free[sl] = ("ST", sch.val("ST"))

        a_front(0)
        a_front(1)
        a_mid(0)
        next_f = [2]
        next_m = [1]
        next_a = [0]

        def maybe_a():
            if next_a[0] < NT:
                a_lo(next_a[0])
                next_a[0] += 1
                if next_m[0] < NT:
                    a_mid(next_m[0])
                    next_m[0] += 1
                if next_f[0] < NT:
                    a_front(next_f[0])
                    next_f[0] += 1

        blk_done = [-1]
        done_c1 = [0]
        cmid_done = [0]

        def maybe_c1():
            tm = done_c1[0]
            if tm >= NMAIN or (WARM + tm * 128) // 512 > blk_done[0]:
                return
            stage_c1(tm)
            if tm % 2 == 1:
                stage_cmid(tm // 2)
                cmid_done[0] += 1
            done_c1[0] += 1

        c2_next = [0]

        def maybe_c2():
            if not R[1]:
                return
            if c2_next[0] >= NMAIN or c2_next[0] >= 2 * cmid_done[0]:
                return
            stage_c2(c2_next[0])
            c2_next[0] += 1

        def router_part1():
            T_QDONE[0] = sch.val("T")
            qa3 = qall[:].rearrange("p (g j) -> p g j", j=3 * E)
            q3 = q16[:].rearrange("p (g j) -> p g j", j=E)
            m3 = m16[:].rearrange("p (g j) -> p g j", j=E)
            e3 = ex16[:].rearrange("p (g j) -> p g j", j=E)
            nx3 = nmx16[:].rearrange("p (g j) -> p g j", j=1)
            sch.add("vector", lambda e: e.scalar_tensor_tensor(
                q3, qa3[:, :, 0:E], 1.0, qa3[:, :, E:2 * E], ALU.mult, ALU.add), inc="V")
            sch.add("vector", lambda e: e.tensor_tensor(
                q3, q3, qa3[:, :, 2 * E:3 * E], ALU.add), inc="V")
            sch.add("vector", lambda e: e.tensor_reduce(nx3, q3, AX.X, ALU.max), inc="V")
            sch.add("vector", lambda e: e.tensor_tensor(
                m3, q3, nx3.broadcast_to([128, NMAIN, E]), ALU.is_equal), inc="V")
            sch.add("vector", lambda e: e.tensor_tensor(
                q3, q3, nx3.broadcast_to([128, NMAIN, E]), ALU.subtract), inc="V")
            v_qn = sch.val("V")
            sch.add("scalar", lambda e: e.activation(ex16[:], q16[:], AF.Exp),
                    waits=[("V", v_qn)], inc="S")
            s_ex = sch.val("S")
            sch.add("vector", lambda e: e.tensor_reduce(
                es16[:].rearrange("p (g j) -> p g j", j=1), e3, AX.X, ALU.add),
                waits=[("S", s_ex)], inc="V")
            sch.add("vector", lambda e: e.reciprocal(gate16[:], es16[:]), inc="V")
            sch.add("vector", lambda e: e.tensor_copy(mball[:], m16[:]), inc="V")
            v_mb = sch.val("V")
            sch.add("tensor", lambda e: e.matmul(P3[0:1, 256:256 + E * NMAIN], onc[:], mball[:],
                                                 start=True, stop=True),
                    waits=[("V", v_mb)], inc="T")
            t_tot = sch.val("T")
            sch.add("vector", lambda e: e.tensor_copy(snapc[:, 0:E], carry[:]),
                    waits=[("T", t_tot)], inc="V")
            sch.add("vector", lambda e: e.tensor_copy(carry[:], P3[0:1, 256:256 + E]), inc="V")
            for tm in range(1, NMAIN):
                sch.add("vector", lambda e, tm=tm: e.tensor_copy(
                    snapc[:, tm * E:(tm + 1) * E], carry[:]), inc="V")
                sch.add("vector", lambda e, tm=tm: e.tensor_tensor(
                    carry[:], carry[:], P3[0:1, 256 + tm * E:256 + (tm + 1) * E], ALU.add),
                    inc="V")
            V_COUNT = sch.val("V")
            dma("gpsimd", wesb[:].rearrange("p (c f) -> p c f", f=D),
                w_e.rearrange("(c p) f -> p c f", p=128), "WL",
                waits=[("T", T_QDONE[0]), ("V", V_COUNT), ("WL", WL_WOUT)])
            WLWE[0] = sch.val("WL")
            dma("gpsimd", cc_in.ap(), carry[:], "WL", waits=[("V", V_COUNT)])
            wl_cc = sch.val("WL")

            def coll(e):
                return e.collective_compute("AllGather", ALU.bypass,
                                            replica_groups=[list(range(NC))],
                                            ins=[cc_in.ap().opt()], outs=[cc_out.ap().opt()])
            sch.add("gpsimd", coll, waits=[("WL", wl_cc)], inc="CC")
            dma("gpsimd", gth[:], cc_out.ap(), "WL", waits=[("CC", sch.val("CC"))])
            WLGTH[0] = sch.val("WL")
            for tm in range(NMAIN):
                def mmc(e, tm=tm):
                    return e.matmul(P3[:, 128 + tm * E:128 + (tm + 1) * E], slo[:],
                                    mball[:, tm * E:(tm + 1) * E], start=True, stop=True)
                sch.add("tensor", mmc, inc="T")
            t_cum = sch.val("T")
            for tm in range(NMAIN):
                sch.add("vector", lambda e, tm=tm: e.scalar_tensor_tensor(
                    cnts[:], m16[:, tm * E:(tm + 1) * E], 1.0,
                    P3[:, 128 + tm * E:128 + (tm + 1) * E],
                    ALU.mult, ALU.mult),
                    waits=[("T", t_cum)] if tm == 0 else [], inc="V")
                sch.add("vector", lambda e, tm=tm: e.reduce_sum(
                    oc16[:, tm:tm + 1], cnts[:], axis=AX.X), inc="V")
                v_oc[0] = sch.val("V")

        def router_part2():
            sch.add("tensor", lambda e: e.matmul(P3[0:1, 256:256 + E], sel[:], gth[:],
                                                 start=True, stop=True),
                    waits=[("WL", WLGTH[0])], inc="T")
            t_pre = sch.val("T")
            sch.add("vector", lambda e: e.tensor_copy(prow[:], P3[0:1, 256:256 + E]),
                    waits=[("T", t_pre)], inc="V")
            v_prow = sch.val("V")
            for tm in range(NMAIN):
                sch.add("vector", lambda e, tm=tm: e.tensor_tensor(
                    cprow[:, tm * E:(tm + 1) * E], snapc[:, tm * E:(tm + 1) * E],
                    prow[:], ALU.add),
                    waits=[("V", v_prow)] if tm == 0 else [], inc="V")
            v_cpr = sch.val("V")
            sch.add("tensor", lambda e: e.matmul(P3[:, 128:256], onr32[:], cprow[:],
                                                 start=True, stop=True),
                    waits=[("V", v_cpr)], inc="T")
            t_bc = sch.val("T")
            for tm in range(NMAIN):
                sch.add("vector", lambda e, tm=tm: e.scalar_tensor_tensor(
                    cnts[:], m16[:, tm * E:(tm + 1) * E], 1.0,
                    P3[:, 128 + tm * E:128 + (tm + 1) * E], ALU.mult, ALU.mult),
                    waits=[("T", t_bc)] if tm == 0 else [], inc="V")
                sch.add("vector", lambda e, tm=tm: e.reduce_sum(
                    op16[:, tm:tm + 1], cnts[:], axis=AX.X), inc="V")
            sch.add("vector", lambda e: e.tensor_tensor(sum16[:], oc16[:], op16[:], ALU.add),
                    inc="V")
            sch.add("vector", lambda e: e.tensor_scalar(keep16[:], sum16[:], CAP, None,
                                                        ALU.is_lt), inc="V")
            sch.add("vector", lambda e: e.tensor_tensor(kg16[:], keep16[:], gate16[:],
                                                        ALU.mult), inc="V")
            VKG[0] = sch.val("V")

        R = [False, False]
        T_QDONE = [0]
        WLGTH = [0]
        v_oc = [0]
        chunks = [(bi, c) for bi in range(5) for c in range(NCH)]
        ci = [0]

        def maybe_b():
            if ci[0] < len(chunks):
                bi, c = chunks[ci[0]]
                off, ln = BLKS[bi]
                if (off + ln - 1) // 128 < next_m[0]:
                    stage_b(bi, c)
                    ci[0] += 1
                    if c == NCH - 1:
                        blk_done[0] = bi
                    return True
            return False

        tog = [0]
        while next_a[0] < NT:
            maybe_b()
            maybe_a()
        router_part1()
        R[0] = True
        while ci[0] < len(chunks):
            maybe_b()
            if not R[1] and blk_done[0] >= 2:
                router_part2()
                R[1] = True
            if tog[0] % 2 == 0:
                maybe_c1()
            elif R[1]:
                maybe_c2()
            tog[0] += 1
        if not R[1]:
            router_part2()
            R[1] = True
        while done_c1[0] < NMAIN or c2_next[0] < NMAIN:
            maybe_c1()
            maybe_c2()

        sch.add("sync", lambda e: e.nop(), waits=[("ST", n_store[0])])

        sems = {}
        for s_name in sch.sem_names():
            sems[s_name] = es.enter_context(nc.semaphore(s_name.replace(".", "_")))
        block = es.enter_context(nc.Block())
        block.sync(lambda e: sch.emit_body(nc, sems, "sync"))
        block.gpsimd(lambda e: sch.emit_body(nc, sems, "gpsimd"))
        block.scalar(lambda e: sch.emit_body(nc, sems, "scalar"))
        block.vector(lambda e: sch.emit_body(nc, sems, "vector"))
        block.tensor(lambda e: sch.emit_body(nc, sems, "tensor"))

    return nc


_NC_CACHE = None


def kernel(L0_attn_x, L0_attn_W_in, L0_attn_b_in, L0_attn_log_A, L0_attn_B_p,
           L0_attn_C_p, L0_attn_W_out, L0_attn_b_out, L0_attn_gamma, L0_attn_beta,
           L0_ffn_Wg, L0_ffn_We, L0_ffn_be):
    global _NC_CACHE
    x = np.ascontiguousarray(np.asarray(L0_attn_x, np.float32).reshape(16384, D))

    if _NC_CACHE is None:
        _NC_CACHE = build_nc()
    nc = _NC_CACHE

    common = {
        "w_in": np.ascontiguousarray(np.asarray(L0_attn_W_in, np.float32)),
        "w_out": np.ascontiguousarray(np.asarray(L0_attn_W_out, np.float32)),
        "w_e": np.ascontiguousarray(np.asarray(L0_ffn_We, np.float32)),
        "w_g": np.ascontiguousarray(np.asarray(L0_ffn_Wg, np.float32)),
        "b_in": np.asarray(L0_attn_b_in, np.float32).reshape(1, D),
        "b_in_c": np.ascontiguousarray(np.asarray(L0_attn_b_in, np.float32).reshape(E, 128).T),
        "b_out": np.asarray(L0_attn_b_out, np.float32).reshape(1, D),
        "b_e": np.asarray(L0_ffn_be, np.float32).reshape(1, D),
        "log_a": np.ascontiguousarray(np.asarray(L0_attn_log_A, np.float32).reshape(E, 128).T),
        "b_p": np.ascontiguousarray(np.asarray(L0_attn_B_p, np.float32).reshape(E, 128).T),
        "c_p": np.ascontiguousarray(np.asarray(L0_attn_C_p, np.float32).reshape(E, 128).T),
        "gam_bc": np.ascontiguousarray(np.broadcast_to(
            np.asarray(L0_attn_gamma, np.float32).reshape(1, D), (128, D))),
        "bet_bc": np.ascontiguousarray(np.broadcast_to(
            np.asarray(L0_attn_beta, np.float32).reshape(1, D), (128, D))),
        "identb": np.eye(128, dtype=np.float32).astype(BF),
        "slob": np.triu(np.ones((128, 128), np.float32), 1).astype(BF),
        "onescol_b": np.ones((128, 1), np.float32).astype(BF),
        "onesrow_b": np.ones((1, 128), np.float32).astype(BF),
        "onesrow32": np.ones((1, 128), np.float32),
    }
    in_maps = []
    for k in range(NC):
        start = k * SLICE
        xsl = np.zeros((TOK, D), np.float32)
        lo = max(0, start - WARM)
        xsl[WARM - (start - lo):] = x[lo:start + SLICE]
        brow = np.ones((1, TOK), np.float32)
        if k == 0:
            brow[0, :WARM] = 0.0
        selv = np.zeros((8, 1), np.float32)
        selv[:k, 0] = 1.0
        im = dict(common)
        im["xs"] = xsl
        im["browb"] = brow.astype(BF)
        im["sel8"] = selv
        in_maps.append(im)

    res = run_bass_kernel_spmd(nc, in_maps, core_ids=list(range(NC)))
    out = np.concatenate([res.results[k]["y_out"] for k in range(NC)], axis=0)
    return np.ascontiguousarray(out).reshape(1, 16384, D)
